# version 18
# baseline (speedup 1.0000x reference)
import numpy as np
import ml_dtypes
from contextlib import ExitStack

import concourse.bass as bass
import concourse.mybir as mybir
from concourse.bass_utils import run_bass_kernel_spmd

F32 = mybir.dt.float32
BF16 = mybir.dt.bfloat16
AF = mybir.ActivationFunctionType
ALU = mybir.AluOpType
AX = mybir.AxisListType
bfdt = ml_dtypes.bfloat16

D, E, NST, DDLT, DCONV = 1024, 2048, 16, 64, 4
NEXP, HID = 8, 4096
T = 2048
NCORE = 8
ET, DC, HM = 16, 8, 32
NQ, QT = 4, 512
NG, GT = 2, 1024
EPS = 1e-5
TCOL = 2051
RB = 3

TAPS = False
STOP = 99
ENGS = ("sync", "tensor", "vector", "scalar", "gpsimd")
DMA_FAMS = ("la", "lw", "ob", "sd", "wo")
DMA_K = 4
SEMS = (["t0", "v0", "s0", "g0", "cc"]
        + [f"{f}{j}" for f in DMA_FAMS for j in range(DMA_K)])


class Sched:
    def __init__(self):
        self.q = {e: [] for e in ENGS}
        self.cnt = {s: 0 for s in SEMS}
        self.lastwait = {e: {} for e in ENGS}
        self.fam = {f: {"i": 0, "uses": [0] * DMA_K} for f in DMA_FAMS}
        self._h = None

    def dma_sem_next(self, name):
        f = self.fam[name]
        j = f["i"] % DMA_K
        f["i"] += 1
        key = f"{name}{j}"
        prev = 16 * f["uses"][j]
        f["uses"][j] += 1
        return key, prev, 16 * f["uses"][j]

    def fam_mark(self, name):
        f = self.fam[name]
        return [(f"{name}{j}", 16 * f["uses"][j])
                for j in range(DMA_K) if f["uses"][j]]

    def wait_marks(self, eng, marks):
        for k, v in marks:
            self.wait(eng, k, v)

    def waitt(self, eng, token):
        self.wait(eng, token[0], token[1])

    def wait_fam(self, eng, name):
        self.wait_marks(eng, self.fam_mark(name))

    OWN = {"vector": "v0", "scalar": "s0", "gpsimd": "g0", "tensor": "t0"}

    def do(self, eng, fn, inc=None, k=1):
        if eng in ("vector", "scalar", "gpsimd"):
            own = self.OWN[eng]
            self.wait(eng, own, self.cnt[own])
        if inc is not None:
            self.cnt[inc] += k
        self.q[eng].append((fn, inc, k))
        return self.cnt[inc] if inc is not None else None

    def wait(self, eng, sem, val=None, force=False):
        if val is None:
            val = self.cnt[sem]
        if val <= 0:
            return
        if not force and self.lastwait[eng].get(sem, 0) >= val:
            return
        self.lastwait[eng][sem] = max(val, self.lastwait[eng].get(sem, 0))
        self.q[eng].append((lambda e, s=sem, v=val: e.wait_ge(self._h[s], v), None, 1))

    def barrier(self):
        snap = dict(self.cnt)
        for e in ENGS:
            for s, v in snap.items():
                self.wait(e, s, v)

    def replay(self, block):
        def make(items):
            def run(engobj):
                for fn, inc, k in items:
                    ins = fn(engobj)
                    if inc is not None:
                        ins.then_inc(self._h[inc], k)
            return run
        for name in ENGS:
            getattr(block, name)(make(self.q[name]))


P = Sched()


class Buf:

    def __init__(self):
        self.rel = None

    def require(self, eng):
        if self.rel is not None:
            P.wait(eng, self.rel[0], self.rel[1])

    def consumed(self, sem):
        self.rel = (sem, P.cnt[sem])


def build_program():
    global P
    P = Sched()
    nc = bass.Bass("TRN2", target_bir_lowering=False, debug=False)

    def din(name, shape, dt=F32):
        return nc.dram_tensor(name, shape, dt, kind="ExternalInput")

    resid_m = din("resid_m", [T, D])
    resid_h = din("resid_h", [128, D])
    norm_w = din("norm_w", [D])
    in_w = din("in_w", [E, D])
    skip_w = din("skip_w", [E, D])
    out_w = din("out_w", [D, E])
    conv_w = din("conv_w", [E, DCONV])
    conv_b = din("conv_b", [E])
    stack_w = din("stack_w", [96, E])
    wd2_w = din("wd2_w", [E, DDLT])
    wd2_b = din("wd2_b", [E])
    a_log = din("a_log", [E, NST])
    w_d = din("w_d", [E])
    gate_w = din("gate_w", [D, NEXP])
    fc_w = din("fc_w", [NEXP, D, HID])
    proj_w = din("proj_w", [NEXP, HID, D])
    ident_f = din("ident_f", [128, 128])
    ident_b = din("ident_b", [128, 128], BF16)
    oneh_bf = din("oneh_bf", [128, 32 * 128], BF16)
    onem_bf = din("onem_bf", [128, 8 * 128], BF16)
    msel = din("msel", [128, 24])

    out_t = nc.dram_tensor("out", [T, D], F32, kind="ExternalOutput")

    scratch_kind = dict(kind="ExternalOutput") if TAPS else {}
    xn_hbm = nc.dram_tensor("xn_hbm", [DC, 128, T], BF16, **scratch_kind)
    s_hbm = nc.dram_tensor("s_hbm", [DC, 128, T], F32, **scratch_kind)
    acc_hbm = nc.dram_tensor("acc_hbm", [NG, DC, 128, GT], F32, **scratch_kind)
    ss_hbm = nc.dram_tensor("ss_hbm", [ET, 128, T], BF16, **scratch_kind)
    fc_bf = nc.dram_tensor("fc_bf", [NEXP, D, HID], BF16)
    proj_bf = nc.dram_tensor("proj_bf", [NEXP, HID, D], BF16)
    inT_hbm = nc.dram_tensor("inT_hbm", [ET, 128, 1024], BF16)
    skipT_hbm = nc.dram_tensor("skipT_hbm", [ET, 128, 1024], BF16)
    outT_hbm = nc.dram_tensor("outT_hbm", [ET, 128, 1024], BF16)
    cc_in = nc.dram_tensor("cc_in", [128, 512], F32)
    cc_out = nc.dram_tensor("cc_out", [NCORE * 128, 512], F32)

    bank_free = {}
    ps_halves = {}

    def bguard(*banks):
        for b in banks:
            if b in bank_free:
                P.wait("tensor", *bank_free[b])
            for off in (0, 128):
                if (b, off) in ps_halves:
                    P.wait("tensor", *ps_halves[(b, off)])

    def brelease(sem, *banks):
        for b in banks:
            bank_free[b] = (sem, P.cnt[sem])

    with ExitStack() as top:
        def sball(name, shape, dt=F32):
            return top.enter_context(nc.sbuf_tensor(name, shape, dt))

        PS1 = top.enter_context(nc.psum_tensor("PS1", [128, 2048], F32))
        PS2 = top.enter_context(nc.psum_tensor("PS2", [128, 2048], F32))
        P._h = {}
        for _s in list(P.cnt):
            P._h[_s] = top.enter_context(nc.semaphore(_s))

        def bF(i, n=512):
            t, j = (PS1, i) if i < 4 else (PS2, i - 4)
            return (t[:, j * 512:j * 512 + n])

        def bB(i, n=1024):
            t, j = (PS1, i) if i < 4 else (PS2, i - 4)
            return t[:, :].bitcast(BF16)[:, j * 1024:j * 1024 + n]

        normw_sb = sball("normw_sb", [128, DC])
        convw_sb = sball("convw_sb", [128, ET * DCONV])
        convb_sb = sball("convb_sb", [128, ET])
        wd2b_sb = sball("wd2b_sb", [128, ET])
        wd_sb = sball("wd_sb", [128, ET])
        a_sb = sball("a_sb", [128, ET * NST])
        gw_sb = sball("gw_sb", [128, DC * NEXP])
        idf_sb = sball("idf_sb", [128, 128], F32)
        idb_sb = sball("idb_sb", [128, 128], BF16)
        oneh_sb = sball("oneh_sb", [128, 32 * 128], BF16)
        onem_sb = sball("onem_sb", [128, 8 * 128], BF16)
        msel_sb = sball("msel_sb", [128, 24], F32)
        m_fm = sball("m_fm", [128, T], BF16)
        wd2T = sball("wd2T", [128, ET * 128], BF16)
        dlow_bf = sball("dlow_bf", [128, T], BF16)
        bc_bf = sball("bc_bf", [128, T], BF16)
        dtot = sball("dtot", [128, ET], F32)
        hin_sb = sball("hin_sb", [128, 256], F32)
        lncol = sball("lncol", [128, 12], F32)
        lnsq = sball("lnsq", [128, 1], F32)
        lnact = sball("lnact", [128, 2], F32)
        onesf = sball("onesf", [128, 512], F32)
        mid = ExitStack()
        hcarry = mid.enter_context(nc.sbuf_tensor("hcarry", [128, ET * NST], F32))
        state_sb = mid.enter_context(nc.sbuf_tensor("state_sb", [128, 512], F32))
        stackT = mid.enter_context(nc.sbuf_tensor("stackT", [128, ET * 128], BF16))
        xrms = mid.enter_context(nc.sbuf_tensor("xrms", [128, DC * TCOL], BF16))

        def dma(eng, dst, src, fam, accum=None, slow=False):
            key, prev, tot = P.dma_sem_next(fam)
            if prev:
                P.wait(eng, key, prev)
            if accum is None:
                P.do(eng, lambda e, d=dst, s=src, sl=slow: e.dma_start(
                    out=d, in_=s, allow_slow_non_contiguous=sl), key, 16)
            else:
                P.do(eng, lambda e, d=dst, s=src, a=accum, sl=slow: e.dma_start(
                    out=d, in_=s, accum_op=a, allow_slow_non_contiguous=sl), key, 16)
            return (key, tot)

        sadd_last = {}

        def dma_add(key, dst, src, first):
            if key in sadd_last:
                P.waitt("gpsimd", sadd_last[key])
            v = dma("gpsimd", dst, src, "sd", accum=None if first else ALU.add)
            sadd_last[key] = v
            return v

        tr_pp = [0]

        def tap(name, sbuf_ap, dt):
            if not TAPS:
                return
            tt = nc.dram_tensor("tap_" + name, list(sbuf_ap.shape), dt,
                                kind="ExternalOutput")
            P.wait("sync", "v0")
            P.wait("sync", "s0")
            P.wait("sync", "g0")
            dma("sync", tt.ap(), sbuf_ap, "ob")
            P.barrier()

        def transpose_to(srcslice, bank, dstslice, bf_in=True, scale=None, pp=False):
            if pp:
                off = 128 * (tr_pp[0] % 2)
                tr_pp[0] += 1
            else:
                off = 0
            reg = (bank, off)
            if reg in ps_halves:
                P.wait("tensor", *ps_halves[reg])
            if bank in bank_free:
                P.wait("tensor", *bank_free[bank])
            tq = P.do("tensor", lambda e, s=srcslice, b=bank, o=off, bi=bf_in: e.matmul(
                bF(b, 512)[:, o:o + 128], s, (idb_sb[:, :] if bi else idf_sb[:, :]),
                start=True, stop=True), "t0")
            P.wait("vector", "t0", tq)
            src_ps = bF(bank, 512)[:, off:off + 128]
            if scale is None:
                P.do("vector", lambda e, d=dstslice, s=src_ps: e.tensor_copy(d, s), "v0")
            else:
                P.do("vector", lambda e, d=dstslice, s=src_ps, sc=scale:
                     e.tensor_scalar_mul(d, s, sc), "v0")
            ps_halves[reg] = ("v0", P.cnt["v0"])

        def fin():
            blk = top.enter_context(nc.Block())
            P.replay(blk)
            return nc

        dma("sync", normw_sb[:, :], norm_w.ap().rearrange("(c p) -> p c", p=128), "la", slow=True)
        dma("sync", convw_sb[:, :].rearrange("p (c k) -> p c k", c=ET),
    conv_w.ap().rearrange("(c p) k -> p c k", p=128), "la")
        dma("sync", convb_sb[:, :], conv_b.ap().rearrange("(c p) -> p c", p=128), "la", slow=True)
        dma("sync", wd2b_sb[:, :], wd2_b.ap().rearrange("(c p) -> p c", p=128), "la", slow=True)
        dma("sync", wd_sb[:, :], w_d.ap().rearrange("(c p) -> p c", p=128), "la", slow=True)
        dma("sync", a_sb[:, :].rearrange("p (c n) -> p c n", c=ET),
    a_log.ap().rearrange("(c p) n -> p c n", p=128), "la")
        dma("sync", gw_sb[:, :].rearrange("p (c g) -> p c g", c=DC),
    gate_w.ap().rearrange("(c p) g -> p c g", p=128), "la")
        dma("sync", idf_sb[:, :], ident_f.ap(), "la")
        dma("sync", idb_sb[:, :], ident_b.ap(), "la")
        dma("sync", oneh_sb[:, :], oneh_bf.ap(), "la")
        dma("sync", onem_sb[:, :], onem_bf.ap(), "la")
        dma("sync", msel_sb[:, :], msel.ap(), "la")
        la_consts = P.fam_mark("la")

        for buf, val in [(m_fm[:, :], 0.0), (dlow_bf[:, :], 0.0), (bc_bf[:, :], 0.0),
                         (onesf[:, :], 1.0)]:
            P.do("vector", lambda e, b=buf, v=val: e.memset(b, v), "v0")

        P.wait_marks("scalar", la_consts)
        P.do("scalar", lambda e: e.activation(a_sb[:, :], a_sb[:, :], AF.Exp), "s0")
        P.wait("vector", "s0")
        P.do("vector", lambda e: e.tensor_scalar_mul(a_sb[:, :], a_sb[:, :], -1.0), "v0")

        with ExitStack() as ph:
            def sb0(name, shape, dt=F32):
                return ph.enter_context(nc.sbuf_tensor(name, shape, dt))
            rtile = sb0("rtile", [128, D])
            sqsc = sb0("sqsc", [128, D])
            xn_tm = sb0("xn_tm", [128, D])
            xr_tm = sb0("xr_tm", [128, D], BF16)
            xnf_st = sb0("xnf_st", [128, 128])
            xn_st = sb0("xn_st", [128, D], BF16)
            z8 = sb0("z8", [128, 8])
            zc = sb0("zc", [128, 3 * 16])
            eqs = sb0("eqs", [128, 2 * 8 * 16])
            w12 = sb0("w12", [128, 3 * 16])
            mpad = sb0("mpad", [128, 128])
            t2 = sb0("t2", [128, 8])
            stk_raw = sb0("stk_raw", [128, E], BF16)
            wd2_raw = sb0("wd2_raw", [128, ET * 128], BF16)
            wstg = sb0("wstg", [128, 2 * 1024], BF16)
            wout = sb0("wout", [128, 2 * 1024], BF16)

            P.do("vector", lambda e: e.memset(mpad[:, :], 0.0), "v0")
            P.do("vector", lambda e: e.memset(stk_raw[96:128, :], 0.0), "v0")
            vq_ms = P.do("vector", lambda e: e.memset(wd2_raw[:, :], 0.0), "v0")

            dma("gpsimd", stk_raw[0:96, :], stack_w.ap(), "lw")
            P.wait("gpsimd", "v0", vq_ms)
            dma("gpsimd", wd2_raw[:, :].rearrange("p (c k) -> p c k", c=ET)[:, :, 0:DDLT],
                wd2_w.ap().rearrange("(c p) k -> p c k", p=128), "lw")
            lw_w = P.fam_mark("lw")

            P.wait_marks("tensor", lw_w)
            P.wait_marks("tensor", la_consts)
            P.wait("tensor", "v0", vq_ms)
            for srcb, dstb in [(stk_raw, stackT), (wd2_raw, wd2T)]:
                for et in range(ET):
                    transpose_to(srcb[:, et * 128:(et + 1) * 128], 7,
                                 dstb[:, et * 128:(et + 1) * 128])

            wjobs = []
            for et in range(ET):
                wjobs.append((in_w.ap()[et * 128:(et + 1) * 128, :], inT_hbm, et))
            for et in range(ET):
                wjobs.append((skip_w.ap()[et * 128:(et + 1) * 128, :], skipT_hbm, et))
            for et in range(ET):
                wjobs.append((out_w.ap()[:, et * 128:(et + 1) * 128].rearrange(
                    "(c p) q -> p c q", p=128), outT_hbm, et))
            wj_state = {"k": 0, "in": [0, 0], "out": [0, 0]}

            def wjob_load(k):
                h = k % 2
                src = wjobs[k][0]
                if wj_state["out"][h]:
                    P.wait("gpsimd", "t0", wj_state["in"][h])
                dst = wstg[:, h * 1024:(h + 1) * 1024]
                if len(src.shape) == 3:
                    dst = dst.rearrange("p (c q) -> p c q", c=DC)
                return dma("gpsimd", dst, src, "lw")

            def wjob_step():
                k = wj_state["k"]
                if k >= len(wjobs):
                    return
                wj_state["k"] = k + 1
                if k == 0:
                    wjob_load(0)
                h = k % 2
                P.wait_fam("tensor", "lw")
                if k + 1 < len(wjobs):
                    wjob_load(k + 1)
                _, dsth, et = wjobs[k]
                if wj_state["out"][h]:
                    P.waitt("scalar", wj_state["out"][h])
                for j in range(8):
                    bk = 3 if j % 2 == 0 else 6
                    bguard(bk)
                    tq = P.do("tensor", lambda e, j=j, h=h, bk=bk: e.matmul(
                        bF(bk, 128), wstg[:, h * 1024 + j * 128:h * 1024 + (j + 1) * 128],
                        idb_sb[:, :], start=True, stop=True), "t0")
                    P.wait("scalar", "t0", tq)
                    P.do("scalar", lambda e, j=j, h=h, bk=bk: e.activation(
                        wout[:, h * 1024 + j * 128:h * 1024 + (j + 1) * 128],
                        bF(bk, 128), AF.Copy), "s0")
                    brelease("s0", bk)
                wj_state["in"][h] = P.cnt["t0"]
                sq = P.cnt["s0"]
                P.wait("gpsimd", "s0", sq)
                oq = dma("gpsimd", dsth.ap()[et], wout[:, h * 1024:(h + 1) * 1024], "wo")
                wj_state["out"][h] = oq

            xnst_buf = Buf()
            rtile_buf = Buf()
            for ti in range(17):
                real = ti > 0
                i = ti - 1
                for _ in range(3):
                    wjob_step()
                rtile_buf.require("sync")
                if ti == 0:
                    lq = dma("sync", rtile[:, :], resid_h.ap(), "la")
                else:
                    lq = dma("sync", rtile[:, :], resid_m.ap()[i * 128:(i + 1) * 128, :], "la")
                P.waitt("vector", lq)
                P.waitt("scalar", lq)
                P.do("vector", lambda e: e.reduce_sum(
                    lncol[:, 0:1], rtile[:, :], axis=AX.X), "v0")
                P.wait("scalar", "v0")
                P.do("scalar", lambda e: e.activation(
                    sqsc[:, :], rtile[:, :], AF.Square, accum_out=lnsq[:, 0:1]), "s0")
                P.wait("vector", "s0")
                P.do("vector", lambda e: e.tensor_scalar_mul(
                    lncol[:, 2:3], lncol[:, 0:1], 1.0 / D), "v0")
                P.do("vector", lambda e: e.tensor_scalar_mul(
                    lncol[:, 3:4], lnsq[:, 0:1], 1.0 / D), "v0")
                P.wait("vector", "v0", force=True)
                P.do("vector", lambda e: e.tensor_mul(
                    lncol[:, 4:5], lncol[:, 2:3], lncol[:, 2:3]), "v0")
                P.do("vector", lambda e: e.tensor_sub(
                    lncol[:, 4:5], lncol[:, 3:4], lncol[:, 4:5]), "v0")
                P.do("vector", lambda e: e.tensor_scalar_add(
                    lncol[:, 5:6], lncol[:, 4:5], EPS), "v0")
                P.do("vector", lambda e: e.reciprocal(
                    lncol[:, 6:7], lncol[:, 5:6]), "v0")
                P.do("vector", lambda e: e.tensor_mul(
                    lncol[:, 7:8], lncol[:, 4:5], lncol[:, 6:7]), "v0")
                P.do("vector", lambda e: e.tensor_scalar_add(
                    lncol[:, 7:8], lncol[:, 7:8], EPS), "v0")
                P.wait("scalar", "v0")
                P.do("scalar", lambda e: e.activation(
                    lnact[:, 0:1], lncol[:, 5:6], AF.Ln), "s0")
                P.do("scalar", lambda e: e.activation(
                    lnact[:, 0:1], lnact[:, 0:1], AF.Exp, scale=-0.5), "s0")
                P.do("scalar", lambda e: e.activation(
                    lnact[:, 1:2], lncol[:, 7:8], AF.Ln), "s0")
                P.do("scalar", lambda e: e.activation(
                    lnact[:, 1:2], lnact[:, 1:2], AF.Exp, scale=-0.5), "s0")
                P.wait("vector", "s0")
                P.do("vector", lambda e: e.tensor_mul(
                    lncol[:, 10:11], lncol[:, 2:3], lnact[:, 0:1]), "v0")
                P.do("vector", lambda e: e.tensor_scalar_mul(
                    lncol[:, 10:11], lncol[:, 10:11], -1.0), "v0")
                P.do("vector", lambda e: e.tensor_mul(
                    lncol[:, 11:12], lnact[:, 0:1], lnact[:, 1:2]), "v0")
                P.do("vector", lambda e: e.tensor_mul(
                    lncol[:, 0:1], lncol[:, 2:3], lncol[:, 11:12]), "v0")
                P.do("vector", lambda e: e.tensor_scalar_mul(
                    lncol[:, 0:1], lncol[:, 0:1], -1.0), "v0")
                if ti == 1:
                    tap("rtile1", rtile[:, :], F32)
                    tap("lncol1", lncol[:, :], F32)
                P.wait("vector", "v0", force=True)
                P.do("vector", lambda e: e.tensor_scalar(
                    xn_tm[:, :], rtile[:, :], lnact[:, 0:1], lncol[:, 10:11],
                    op0=ALU.mult, op1=ALU.add), "v0")
                vq = P.do("vector", lambda e: e.tensor_scalar(
                    xr_tm[:, :], rtile[:, :], lncol[:, 11:12], lncol[:, 0:1],
                    op0=ALU.mult, op1=ALU.add), "v0")
                rtile_buf.consumed("v0")
                if ti == 1:
                    tap("xntm1", xn_tm[:, :], F32)
                    tap("xrtm1", xr_tm[:, :], BF16)
                P.wait("tensor", "v0", vq)
                xnst_buf.require("vector")
                for dc in range(DC):
                    bguard(2)
                    tq = P.do("tensor", lambda e, dc=dc: e.matmul(
                        bF(2, 128), xn_tm[:, dc * 128:(dc + 1) * 128],
                        idf_sb[:, :], start=True, stop=True), "t0")
                    P.wait("vector", "t0", tq)
                    if real:
                        P.do("vector", lambda e: e.tensor_copy(
                            xnf_st[:, :], bF(2, 128)), "v0")
                    P.do("vector", lambda e, dc=dc: e.tensor_copy(
                        xn_st[:, dc * 128:(dc + 1) * 128], bF(2, 128)), "v0")
                    brelease("v0", 2)
                    if real:
                        P.wait("tensor", "v0")
                        if dc == 0:
                            bguard(0)
                        P.do("tensor", lambda e, dc=dc: e.matmul(
                            bF(0, 8), xnf_st[:, :], gw_sb[:, dc * 8:(dc + 1) * 8],
                            start=(dc == 0), stop=(dc == DC - 1)), "t0")
                tq_z = P.cnt["t0"]
                for dc in range(DC):
                    bguard(7)
                    tq = P.do("tensor", lambda e, dc=dc: e.matmul(
                        bF(7, 128), xr_tm[:, dc * 128:(dc + 1) * 128],
                        idb_sb[:, :], start=True, stop=True), "t0")
                    P.wait("vector", "t0", tq)
                    if ti == 0:
                        P.do("vector", lambda e, dc=dc: e.tensor_scalar_mul(
                            xrms[:, dc * TCOL:dc * TCOL + 3],
                            bF(7, 128)[:, 125:128], normw_sb[:, dc:dc + 1]), "v0")
                    else:
                        cb = RB + i * 128
                        P.do("vector", lambda e, dc=dc, cb=cb: e.tensor_scalar_mul(
                            xrms[:, dc * TCOL + cb:dc * TCOL + cb + 128],
                            bF(7, 128), normw_sb[:, dc:dc + 1]), "v0")
                    brelease("v0", 7)
                if real:
                    P.wait("sync", "v0")
                    xq = dma("sync",
                             xn_hbm.ap()[:, :, i * 128:(i + 1) * 128].rearrange(
                                 "c p t -> p c t"),
                             xn_st[:, :].rearrange("p (c t) -> p c t", c=DC), "ob")
                    xnst_buf.rel = xq
                    P.wait("vector", "t0", tq_z)
                    P.do("vector", lambda e: e.tensor_copy(z8[:, :], bF(0, 8)), "v0")
                    brelease("v0", 0)
                    c0, e1c = 3 * i, 16 * i
                    P.do("vector", lambda e, c0=c0: e.reduce_max(
                        zc[:, c0:c0 + 1], z8[:, :], axis=AX.X), "v0")
                    P.wait("vector", "v0", force=True)
                    P.do("vector", lambda e, c0=c0, e1c=e1c: e.tensor_single_scalar(
                        eqs[:, e1c:e1c + 8], z8[:, :], zc[:, c0:c0 + 1],
                        op=ALU.is_equal), "v0")
                    P.do("vector", lambda e, e1c=e1c: e.scalar_tensor_tensor(
                        z8[:, :], eqs[:, e1c:e1c + 8], -1e30, z8[:, :],
                        op0=ALU.mult, op1=ALU.add), "v0")
                    P.do("vector", lambda e, c0=c0: e.reduce_max(
                        zc[:, c0 + 1:c0 + 2], z8[:, :], axis=AX.X), "v0")
                    P.wait("vector", "v0", force=True)
                    P.do("vector", lambda e, c0=c0, e1c=e1c: e.tensor_single_scalar(
                        eqs[:, e1c + 8:e1c + 16], z8[:, :], zc[:, c0 + 1:c0 + 2],
                        op=ALU.is_equal), "v0")
                    P.do("vector", lambda e, c0=c0: e.tensor_sub(
                        zc[:, c0 + 2:c0 + 3], zc[:, c0 + 1:c0 + 2], zc[:, c0:c0 + 1]), "v0")

            while wj_state["k"] < len(wjobs):
                wjob_step()
            P.wait_fam("gpsimd", "wo")

            P.wait("scalar", "v0")
            P.do("scalar", lambda e: e.activation(
                w12[:, 0:16], zc[:, 2:48:3], AF.Exp), "s0")
            P.wait("vector", "s0")
            P.do("vector", lambda e: e.tensor_scalar_add(
                w12[:, 0:16], w12[:, 0:16], 1.0), "v0")
            P.do("vector", lambda e: e.reciprocal(w12[:, 16:32], w12[:, 0:16]), "v0")
            P.do("vector", lambda e: e.tensor_scalar(
                w12[:, 32:48], w12[:, 16:32], -1.0, 1.0,
                op0=ALU.mult, op1=ALU.add), "v0")
            P.wait("vector", "v0", force=True)
            for i in range(16):
                e1c = 16 * i
                P.do("vector", lambda e, e1c=e1c: e.tensor_mul(
                    t2[:, :], eqs[:, e1c + 8:e1c + 16], eqs[:, e1c:e1c + 8]), "v0")
                P.do("vector", lambda e, e1c=e1c: e.tensor_sub(
                    t2[:, :], eqs[:, e1c + 8:e1c + 16], t2[:, :]), "v0")
                P.do("vector", lambda e, i=i: e.tensor_scalar_mul(
                    t2[:, :], t2[:, :], w12[:, 32 + i:33 + i]), "v0")
                vq = P.do("vector", lambda e, e1c=e1c, i=i: e.scalar_tensor_tensor(
                    mpad[:, 0:8], eqs[:, e1c:e1c + 8], w12[:, 16 + i:17 + i], t2[:, :],
                    op0=ALU.mult, op1=ALU.add), "v0")
                P.wait("tensor", "v0", vq)
                bguard(1)
                tq = P.do("tensor", lambda e: e.matmul(
                    bF(1, 128), mpad[:, :], idf_sb[:, :], start=True, stop=True), "t0")
                P.wait("vector", "t0", tq)
                P.do("vector", lambda e, i=i: e.tensor_copy(
                    m_fm[0:8, i * 128:(i + 1) * 128], bF(1, 128)[0:8, :]), "v0")
                brelease("v0", 1)

        tap("xrms", xrms[:, :], BF16)
        tap("mfm", m_fm[:, :], BF16)
        P.barrier()
        if STOP == 1:
            return fin()

        with ExitStack() as ph:
            def sbq(name, shape, dt=F32):
                return ph.enter_context(nc.sbuf_tensor(name, shape, dt))
            x_all = sbq("x_all", [128, ET * QT], BF16)
            d2 = sbq("d2", [128, 2 * QT], BF16)
            dsc_e = sbq("dsc_e", [128, QT])
            dsc_f = sbq("dsc_f", [128, QT])
            y_all = sbq("y_all", [128, ET * QT], BF16)
            bint = sbq("bint", [128, 16 * QT], BF16)
            cint = sbq("cint", [128, 16 * QT], BF16)
            hint = sbq("hint", [128, 16 * QT], BF16)
            wTq = sbq("wTq", [128, 2 * 1024], BF16)
            skT = sbq("skT", [128, 4 * 1024], BF16)
            wT2b = sbq("wT2b", [128, 4 * 1024], BF16)
            x_in = sbq("x_in", [128, 2 * 515])
            sil_e = sbq("sil_e", [128, 2 * 512])
            sil_f = sbq("sil_f", [128, 512])
            abar = sbq("abar", [128, 3 * 512], BF16)
            ubuf = sbq("ubuf", [128, 3 * 512], BF16)
            dxb = sbq("dxb", [128, 2 * 512], BF16)
            ss_et = sbq("ss_et", [128, 512], BF16)
            y2_et = sbq("y2_et", [128, 4 * 512], BF16)
            stage_s = sbq("stage_s", [128, 2048])

            sset_buf, stage_buf, y2_buf = Buf(), Buf(), Buf()
            pc_jobs = []
            for ex in range(NEXP):
                for i in range(8):
                    pc_jobs.append((fc_w.ap()[ex, i * 128:(i + 1) * 128, :],
                                    fc_bf.ap()[ex, i * 128:(i + 1) * 128, :]))
                    pc_jobs.append((proj_w.ap()[ex, i * 512:(i + 1) * 512, :],
                                    proj_bf.ap()[ex, i * 512:(i + 1) * 512, :]))
            pc_k = [0]

            def pc_step(n=1):
                for _ in range(n):
                    if pc_k[0] < len(pc_jobs):
                        src, dst = pc_jobs[pc_k[0]]
                        pc_k[0] += 1
                        dma("gpsimd", dst, src, "lw")
            wq_tok = [0, 0]
            wq_free = [0, 0]

            def wq_load(et):
                h = et % 2
                if wq_free[h]:
                    P.wait("sync", "t0", wq_free[h])
                wq_tok[h] = dma("sync", wTq[:, h * 1024:(h + 1) * 1024],
                                inT_hbm.ap()[et], "la")

            for q in range(NQ):
                base = RB + q * QT - 3
                qc = q * QT
                def stage1_mm(k, base=base):
                    h = k % 2
                    bp = (4, 5) if h == 0 else (2, 3)
                    P.waitt("tensor", wq_tok[h])
                    for (c0, cn, bk) in [(base, 512, bp[0]), (base + 451, 64, bp[1])]:
                        bguard(bk)
                        for kc in range(8):
                            P.do("tensor", lambda e, kc=kc, c0=c0, cn=cn, bk=bk,
                                 h=h: e.matmul(
                                bF(bk, cn),
                                wTq[:, h * 1024 + kc * 128:h * 1024 + (kc + 1) * 128],
                                xrms[:, kc * TCOL + c0:kc * TCOL + c0 + cn],
                                start=(kc == 0), stop=(kc == 7)), "t0")
                    mm_tok[h] = P.cnt["t0"]
                    wq_free[h] = P.cnt["t0"]

                def stage1_evac(k):
                    h = k % 2
                    bp = (4, 5) if h == 0 else (2, 3)
                    P.wait("vector", "t0", mm_tok[h])
                    for (cn, xo, bk) in [(512, 0, bp[0]), (64, 451, bp[1])]:
                        P.do("vector", lambda e, cn=cn, xo=xo, bk=bk, h=h: e.tensor_copy(
                            x_in[:, h * 515 + xo:h * 515 + xo + cn], bF(bk, cn)), "v0")
                        brelease("v0", bk)

                wq_load(0)
                wq_load(1)
                mm_tok = [0, 0]
                sil_tok = [0, 0]
                stage1_mm(0)
                stage1_evac(0)
                for et in range(ET):
                    h = et % 2
                    pc_step(1)
                    if et + 2 < ET:
                        wq_load(et + 2)
                    if et + 1 < ET:
                        stage1_mm(et + 1)
                    P.wait("vector", "s0", sil_tok[h])
                    P.do("vector", lambda e, et=et, h=h: e.tensor_scalar_mul(
                        sil_e[:, h * 512:h * 512 + 512], x_in[:, h * 515:h * 515 + 512],
                        convw_sb[:, et * DCONV:et * DCONV + 1]), "v0")
                    for j in range(1, DCONV):
                        P.do("vector", lambda e, j=j, et=et, h=h: e.scalar_tensor_tensor(
                            sil_e[:, h * 512:h * 512 + 512],
                            x_in[:, h * 515 + j:h * 515 + j + 512],
                            convw_sb[:, et * DCONV + j:et * DCONV + j + 1],
                            sil_e[:, h * 512:h * 512 + 512],
                            op0=ALU.mult, op1=ALU.add), "v0")
                    if et + 1 < ET:
                        stage1_evac(et + 1)
                    P.wait("scalar", "v0")
                    sq = P.do("scalar", lambda e, et=et, h=h: e.activation(
                        x_all[:, et * QT:(et + 1) * QT],
                        sil_e[:, h * 512:h * 512 + 512], AF.Silu,
                        bias=convb_sb[:, et:et + 1]), "s0")
                    sil_tok[h] = sq
                P.wait("tensor", "s0")
                bguard(6)
                for et in range(ET):
                    P.do("tensor", lambda e, et=et: e.matmul(
                        bF(6), stackT[:, et * 128:(et + 1) * 128],
                        x_all[:, et * QT:(et + 1) * QT],
                        start=(et == 0), stop=(et == ET - 1)), "t0")
                tq = P.cnt["t0"]
                P.wait("vector", "t0", tq)
                P.do("vector", lambda e, qc=qc: e.tensor_copy(
                    dlow_bf[0:64, qc:qc + 512], bF(6)[0:64, :]), "v0")
                P.do("vector", lambda e, qc=qc: e.tensor_copy(
                    bc_bf[64:96, qc:qc + 512], bF(6)[64:96, :]), "v0")
                brelease("v0", 6)
                P.wait("tensor", "v0")

                for bci in range(2):
                    for n in range(NST):
                        row = 16 * bci + n
                        bk = 6 + (n % 2)
                        bguard(bk)
                        tq = P.do("tensor", lambda e, row=row, qc=qc, bk=bk: e.matmul(
                            bF(bk), oneh_sb[:, row * 128:(row + 1) * 128],
                            bc_bf[:, qc:qc + 512], start=True, stop=True), "t0")
                        P.wait("scalar", "t0", tq)
                        if bci == 0:
                            P.do("scalar", lambda e, n=n, bk=bk: e.activation(
                                bint[:, n * QT:(n + 1) * QT], bF(bk), AF.Copy), "s0")
                        else:
                            P.do("scalar", lambda e, n=n, bk=bk: e.activation(
                                cint[:, n:NST * QT:NST], bF(bk), AF.Copy), "s0")
                        brelease("s0", bk)

                sq_bc = P.cnt["s0"]
                P.wait("vector", "s0", sq_bc)
                P.wait("gpsimd", "s0", sq_bc)
                d2_wr = [0, 0]
                d2_rd = [0, 0]

                def dstage(k, q=q, qc=qc):
                    h = k % 2
                    bguard(0)
                    P.do("tensor", lambda e, k=k, qc=qc: e.matmul(
                        bF(0), wd2T[:, k * 128:(k + 1) * 128],
                        dlow_bf[:, qc:qc + 512], start=True, stop=True), "t0")
                    tqd = P.cnt["t0"]
                    P.wait("scalar", "t0", tqd)
                    P.wait("scalar", "v0")
                    P.do("scalar", lambda e, k=k: e.activation(
                        dsc_e[:, :], bF(0), AF.Abs, bias=wd2b_sb[:, k:k + 1]), "s0")
                    P.do("scalar", lambda e, k=k: e.activation(
                        dsc_f[:, :], bF(0), AF.Relu, bias=wd2b_sb[:, k:k + 1]), "s0")
                    brelease("s0", 0)
                    P.do("scalar", lambda e: e.activation(
                        dsc_e[:, :], dsc_e[:, :], AF.Exp, scale=-1.0), "s0")
                    sqd = P.do("scalar", lambda e: e.activation(
                        dsc_e[:, :], dsc_e[:, :], AF.Ln, bias=1.0), "s0")
                    if d2_rd[h]:
                        P.wait("vector", "s0", max(sqd, d2_rd[h]))
                    else:
                        P.wait("vector", "s0", sqd)
                    P.do("vector", lambda e, h=h: e.tensor_add(
                        d2[:, h * QT:(h + 1) * QT], dsc_e[:, :], dsc_f[:, :]), "v0")
                    P.do("vector", lambda e, h=h: e.reduce_sum(
                        lncol[:, 0:1], d2[:, h * QT:(h + 1) * QT], axis=AX.X), "v0")
                    if q == 0:
                        P.do("vector", lambda e, k=k: e.tensor_copy(
                            dtot[:, k:k + 1], lncol[:, 0:1]), "v0")
                    else:
                        P.do("vector", lambda e, k=k: e.tensor_add(
                            dtot[:, k:k + 1], dtot[:, k:k + 1], lncol[:, 0:1]), "v0")
                    d2_wr[h] = P.cnt["v0"]

                def dxb_mul(k):
                    h2 = k % 2
                    if ub_tok[h2]:
                        P.wait("vector", "g0", ub_tok[h2])
                    vq2[h2] = P.do("vector", lambda e, k=k, h2=h2: e.tensor_mul(
                        dxb[:, h2 * 512:(h2 + 1) * 512], d2[:, h2 * QT:(h2 + 1) * QT],
                        x_all[:, k * QT:(k + 1) * QT]), "v0")

                vq2 = [0, 0]
                ub_tok = [0, 0]
                scan_done = {j: 0 for j in range(3)}
                dstage(0)
                dxb_mul(0)
                for et in range(ET):
                    h = et % 2
                    P.wait("gpsimd", "v0", vq2[h])
                    P.wait("scalar", "v0", d2_wr[h])
                    for n in range(NST):
                        if n == 8 and et + 1 < ET:
                            dstage(et + 1)
                        if n == 12 and et + 1 < ET:
                            dxb_mul(et + 1)
                        pp = (et * NST + n) % 3
                        if scan_done[pp]:
                            P.wait("scalar", "v0", scan_done[pp])
                            P.wait("gpsimd", "v0", scan_done[pp])
                        sq = P.do("scalar", lambda e, et=et, n=n, pp=pp, h=h: e.activation(
                            abar[:, pp * 512:(pp + 1) * 512],
                            d2[:, h * QT:(h + 1) * QT], AF.Exp,
                            scale=a_sb[:, et * NST + n:et * NST + n + 1]), "s0")
                        gq = P.do("gpsimd", lambda e, n=n, pp=pp, h=h: e.tensor_tensor(
                            ubuf[:, pp * 512:(pp + 1) * 512],
                            dxb[:, h * 512:(h + 1) * 512],
                            bint[:, n * QT:(n + 1) * QT], op=ALU.mult), "g0")
                        if n == NST - 1:
                            ub_tok[h] = gq
                        P.wait("vector", "s0", sq)
                        P.wait("vector", "g0", gq)
                        init = 0.0 if q == 0 else hcarry[:, et * NST + n:et * NST + n + 1]
                        scan_done[pp] = P.do(
                            "vector", lambda e, n=n, pp=pp, init=init: e.tensor_tensor_scan(
                                hint[:, n:NST * QT:NST],
                                abar[:, pp * 512:(pp + 1) * 512],
                                ubuf[:, pp * 512:(pp + 1) * 512],
                                init, op0=ALU.mult, op1=ALU.add), "v0")
                    d2_rd[h] = P.cnt["s0"]
                    P.do("vector", lambda e, et=et: e.tensor_copy(
                        hcarry[:, et * NST:(et + 1) * NST],
                        hint[:, (QT - 1) * NST:QT * NST]), "v0")
                    P.do("vector", lambda e: e.tensor_mul(
                        hint[:, :], hint[:, :], cint[:, :]), "v0")
                    P.wait("vector", "s0")
                    vqr = P.do("vector", lambda e: e.tensor_reduce(
                        sil_f[:, :],
                        hint[:, :].rearrange("p (l n) -> p l n", n=16),
                        axis=AX.X, op=ALU.add), "v0")
                    P.wait("scalar", "v0", vqr)
                    P.do("scalar", lambda e, et=et: e.activation(
                        y_all[:, et * QT:(et + 1) * QT], sil_f[:, :], AF.Copy), "s0")

                if q == 0:
                    tap("xall0", x_all[:, :], BF16)
                    tap("yall0", y_all[:, :], BF16)
                for blk in range(4):
                    y2_buf.require("vector")
                    P.wait("sync", "t0")
                    lsk = dma("sync", skT[:, :].rearrange("p (e k) -> p e k", e=4),
                              skipT_hbm.ap()[blk * 4:(blk + 1) * 4].rearrange(
                                  "e p k -> p e k"), "la")
                    lot = dma("sync", wT2b[:, :].rearrange("p (e k) -> p e k", e=4),
                              outT_hbm.ap()[blk * 4:(blk + 1) * 4].rearrange(
                                  "e p k -> p e k"), "la")
                    P.waitt("tensor", lsk)
                    P.waitt("tensor", lot)
                    for ei in range(4):
                        et = blk * 4 + ei
                        pc_step(1)
                        bguard(4)
                        for kc in range(8):
                            P.do("tensor", lambda e, kc=kc, q=q, ei=ei: e.matmul(
                                bF(4), skT[:, ei * 1024 + kc * 128:
                                           ei * 1024 + (kc + 1) * 128],
                                xrms[:, kc * TCOL + RB + q * QT:
                                     kc * TCOL + RB + q * QT + 512],
                                start=(kc == 0), stop=(kc == 7)), "t0")
                        tq = P.cnt["t0"]
                        P.wait("scalar", "t0", tq)
                        sset_buf.require("scalar")
                        P.wait("scalar", "v0")
                        sq = P.do("scalar", lambda e: e.activation(
                            ss_et[:, :], bF(4), AF.Silu), "s0")
                        brelease("s0", 4)
                        P.wait("sync", "s0", sq)
                        sq2 = dma("sync", ss_hbm.ap()[et, :, qc:qc + 512], ss_et[:, :], "ob")
                        sset_buf.rel = sq2
                        P.wait("vector", "s0", sq)
                        P.do("vector", lambda e, et=et: e.scalar_tensor_tensor(
                            dxb[:, 0:512], x_all[:, et * QT:(et + 1) * QT],
                            wd_sb[:, et:et + 1], y_all[:, et * QT:(et + 1) * QT],
                            op0=ALU.mult, op1=ALU.add), "v0")
                        P.do("vector", lambda e, ei=ei: e.tensor_mul(
                            y2_et[:, ei * 512:(ei + 1) * 512], dxb[:, 0:512],
                            ss_et[:, :]), "v0")
                    P.wait("tensor", "v0")
                    for g2c in range(2):
                        bguard(0, 1, 2, 3)
                        for ei in range(4):
                            for dc in range(DC):
                                P.do("tensor", lambda e, ei=ei, dc=dc, g2c=g2c: e.matmul(
                                    PS1[:, dc * 256:(dc + 1) * 256],
                                    wT2b[:, ei * 1024 + dc * 128:ei * 1024 + (dc + 1) * 128],
                                    y2_et[:, ei * 512 + g2c * 256:ei * 512 + (g2c + 1) * 256],
                                    start=(ei == 0 and dc % 2 == 0),
                                    stop=(ei == 3 and dc % 2 == 1)), "t0")
                        tq = P.cnt["t0"]
                        y2_buf.consumed("t0")
                        P.wait("scalar", "t0", tq)
                        stage_buf.require("scalar")
                        vq = P.do("scalar", lambda e: e.activation(
                            stage_s[:, :], PS1[:, :], AF.Copy), "s0")
                        brelease("s0", 0, 1, 2, 3)
                        P.wait("gpsimd", "s0", vq)
                        sdq = dma_add(("s", q, g2c),
                                      s_hbm.ap()[:, :, qc + g2c * 256:qc + (g2c + 1) * 256]
                                      .rearrange("c p t -> p c t"),
                                      stage_s[:, :].rearrange("p (c t) -> p c t", c=DC),
                                      first=(blk == 0))
                        stage_buf.rel = sdq
                P.barrier()

            pc_step(len(pc_jobs))
            P.wait("scalar", "v0")
            for et in range(ET):
                P.do("scalar", lambda e, et=et: e.activation(
                    state_sb[:, et * NST:(et + 1) * NST],
                    a_sb[:, et * NST:(et + 1) * NST], AF.Exp,
                    scale=dtot[:, et:et + 1]), "s0")
            P.wait("vector", "s0")
            P.do("vector", lambda e: e.tensor_copy(
                state_sb[:, 256:512], hcarry[:, :]), "v0")
            vq = P.cnt["v0"]
            P.wait("gpsimd", "v0", vq)
            dma("gpsimd", cc_in.ap(), state_sb[:, :], "lw")
            P.wait_fam("gpsimd", "lw")
            P.do("gpsimd", lambda e: e.collective_compute(
                "AllGather", ALU.bypass,
                replica_groups=[list(range(NCORE))],
                ins=[cc_in.ap().opt()], outs=[cc_out.ap().opt()]), "cc")
            P.wait("gpsimd", "cc", 1)
            tap("state", state_sb[:, :], F32)
            tap("dlow", dlow_bf[:, :], BF16)
            tap("bc", bc_bf[:, :], BF16)

        P.barrier()
        if STOP == 2:
            return fin()

        with ExitStack() as ph:
            gath = ph.enter_context(nc.sbuf_tensor("gath", [128, 8 * 512], F32))
            gscr = ph.enter_context(nc.sbuf_tensor("gscr", [128, 1536], F32))
            lwq = dma("gpsimd", gath[:, :].rearrange("p (j c) -> p j c", j=8),
                      cc_out.ap().rearrange("(j p) c -> p j c", p=128), "lw")
            P.waitt("vector", lwq)
            scr = {"t1": gscr[:, 0:256], "p1": gscr[:, 256:512],
                   "t2": gscr[:, 512:768], "p2": gscr[:, 768:1024],
                   "t3": gscr[:, 1024:1280]}
            for name, ti, off in [("t1", 0, 256), ("p1", 0, 0), ("t2", 1, 256),
                                  ("p2", 1, 0), ("t3", 2, 256)]:
                s = scr[name]
                for j in range(8):
                    if j == 0:
                        P.do("vector", lambda e, s=s, ti=ti, off=off: e.tensor_scalar_mul(
                            s, gath[:, off:off + 256], msel_sb[:, ti * 8:ti * 8 + 1]), "v0")
                    else:
                        P.do("vector", lambda e, s=s, ti=ti, off=off, j=j:
                             e.scalar_tensor_tensor(
                            s, gath[:, j * 512 + off:j * 512 + off + 256],
                            msel_sb[:, ti * 8 + j:ti * 8 + j + 1], s,
                            op0=ALU.mult, op1=ALU.add), "v0")
            P.do("vector", lambda e: e.tensor_mul(
                gscr[:, 1280:1536], scr["p2"], scr["t3"]), "v0")
            P.do("vector", lambda e: e.tensor_add(
                gscr[:, 1280:1536], scr["t2"], gscr[:, 1280:1536]), "v0")
            P.do("vector", lambda e: e.tensor_mul(
                gscr[:, 1280:1536], scr["p1"], gscr[:, 1280:1536]), "v0")
            hin_done = P.do("vector", lambda e: e.tensor_add(
                hin_sb[:, :], scr["t1"], gscr[:, 1280:1536]), "v0")
            tap("gath", gath[:, :], F32)
            tap("hin", hin_sb[:, :], F32)
        P.barrier()
        if STOP == 3:
            return fin()

        mid.close()
        with ExitStack() as ph:
            def sbf(name, shape, dt=F32):
                return ph.enter_context(nc.sbuf_tensor(name, shape, dt))
            hbuf = sbf("hbuf", [128, HM * GT], BF16)
            xn_g = sbf("xn_g", [128, DC * GT], BF16)
            fcs = sbf("fcs", [128, 2 * 4096], BF16)
            prjs = sbf("prjs", [128, 2 * 2048], BF16)
            m_pat = sbf("m_pat", [128, GT], BF16)
            stage_e = sbf("stage_e", [128, 4 * 512])
            ccint = sbf("ccint", [128, 16 * 512], BF16)
            dcor = sbf("dcor", [128, 512])
            Dbuf = sbf("Dbuf", [128, 512])
            gbufs = sbf("gbufs", [128, 16 * 512], BF16)
            tv = sbf("tv", [128, 512])
            tg = sbf("tg", [128, 2 * 512])
            ycor_v = sbf("ycor_v", [128, 512])
            ycor_g = sbf("ycor_g", [128, 512])
            y2c = sbf("y2c", [128, 4 * 512], BF16)
            ssld = sbf("ssld", [128, 512], BF16)
            dqcarry = sbf("dqcarry", [128, ET])
            wTc = sbf("wTc", [128, 4 * 1024], BF16)
            silc = sbf("silc", [128, 512])
            y2c_buf, tbst_buf = Buf(), Buf()
            ssld_buf = Buf()
            fcs_rel = {0: Buf(), 1: Buf()}
            prjs_rel = {0: Buf(), 1: Buf()}
            xng_buf, stgE_buf, hbuf_buf, mpat_buf = Buf(), Buf(), Buf(), Buf()

            def ffn_gen():
                for g2 in range(NG):
                    xng_buf.require("sync")
                    lq = dma("sync", xn_g[:, :].rearrange("p (c t) -> p c t", c=DC),
                             xn_hbm.ap()[:, :, g2 * GT:(g2 + 1) * GT].rearrange(
                                 "c p t -> p c t"), "la")
                    for ex in range(NEXP):
                        gq_m = {}
                        mpat_buf.require("vector")
                        for th in range(2):
                            bguard(4)
                            tq = P.do("tensor", lambda e, ex=ex, g2=g2, th=th: e.matmul(
                                bF(4), onem_sb[:, ex * 128:(ex + 1) * 128],
                                m_fm[:, g2 * GT + th * 512:g2 * GT + (th + 1) * 512],
                                start=True, stop=True), "t0")
                            P.wait("vector", "t0", tq)
                            P.do("vector", lambda e, th=th: e.tensor_copy(
                                m_pat[:, th * 512:(th + 1) * 512], bF(4)), "v0")
                            brelease("v0", 4)
                        yield
                        P.waitt("tensor", lq)
                        hbuf_buf.require("scalar")
                        for ms in range(8):
                            sh = ms % 2
                            fcs_rel[sh].require("sync")
                            lq2 = dma("sync",
                                      fcs[:, sh * 4096:(sh + 1) * 4096]
                                      .rearrange("p (kc mc q) -> p kc mc q", kc=8, mc=4),
                                      fc_bf.ap()[ex, :, ms * 512:(ms + 1) * 512]
                                      .rearrange("(kc p) (mc q) -> p kc mc q",
                                                 p=128, q=128), "la")
                            P.waitt("tensor", lq2)
                            for mc in range(4):
                                m = ms * 4 + mc
                                b0 = 4 + (m % 2) * 2
                                bguard(b0, b0 + 1)
                                for kc in range(8):
                                    for th in range(2):
                                        P.do("tensor", lambda e, kc=kc, mc=mc, th=th,
                                             b0=b0, sh=sh: e.matmul(
                                            bF(b0 + th),
                                            fcs[:, sh * 4096 + (kc * 4 + mc) * 128:
                                                sh * 4096 + (kc * 4 + mc) * 128 + 128],
                                            xn_g[:, kc * GT + th * 512:
                                                 kc * GT + (th + 1) * 512],
                                            start=(kc == 0), stop=(kc == 7)), "t0")
                                tq = P.cnt["t0"]
                                P.wait("scalar", "t0", tq)
                                for th in range(2):
                                    P.do("scalar", lambda e, m=m, th=th, b0=b0:
                                         e.activation(
                                        hbuf[:, m * GT + th * 512:m * GT + (th + 1) * 512],
                                        bF(b0 + th), AF.Gelu), "s0")
                                gq_m[m] = P.cnt["s0"]
                                brelease("s0", b0, b0 + 1)
                                yield
                            fcs_rel[sh].consumed("t0")
                        for th in range(2):
                            for dh in range(2):
                                bguard(4, 5, 6, 7)
                                for hs in range(8):
                                    P.wait("tensor", "s0", gq_m[hs * 4 + 3])
                                    sh = hs % 2
                                    prjs_rel[sh].require("sync")
                                    lq3 = dma("sync",
                                              prjs[:, sh * 2048:(sh + 1) * 2048]
                                              .rearrange("p (hc dc q) -> p hc dc q",
                                                         hc=4, dc=4),
                                              proj_bf.ap()[ex, hs * 512:(hs + 1) * 512,
                                                           dh * 512:(dh + 1) * 512]
                                              .rearrange("(hc p) (dc q) -> p hc dc q",
                                                         p=128, q=128), "la")
                                    P.waitt("tensor", lq3)
                                    for hc in range(4):
                                        m = hs * 4 + hc
                                        for dcg in range(4):
                                            P.do("tensor", lambda e, m=m, th=th, hc=hc,
                                                 dcg=dcg, sh=sh: e.matmul(
                                                bF(4 + dcg),
                                                prjs[:, sh * 2048 + (hc * 4 + dcg) * 128:
                                                     sh * 2048 + (hc * 4 + dcg) * 128 + 128],
                                                hbuf[:, m * GT + th * 512:
                                                     m * GT + (th + 1) * 512],
                                                start=(m == 0), stop=(m == 31)), "t0")
                                    prjs_rel[sh].consumed("t0")
                                    yield
                                tq = P.cnt["t0"]
                                P.wait("vector", "t0", tq)
                                stgE_buf.require("vector")
                                for dc4 in range(4):
                                    P.do("vector", lambda e, dc4=dc4, th=th: e.tensor_mul(
                                        stage_e[:, dc4 * 512:(dc4 + 1) * 512],
                                        bF(4 + dc4),
                                        m_pat[:, th * 512:(th + 1) * 512]), "v0")
                                vq = P.cnt["v0"]
                                brelease("v0", 4, 5, 6, 7)
                                P.wait("gpsimd", "v0", vq)
                                sdq = dma_add(("acc", g2, th, dh),
                                              acc_hbm.ap()[g2, dh * 4:(dh + 1) * 4, :,
                                                           th * 512:(th + 1) * 512]
                                              .rearrange("c p t -> p c t"),
                                              stage_e[:, :].rearrange(
                                                  "p (c t) -> p c t", c=4),
                                              first=(ex == 0))
                                stgE_buf.rel = sdq
                                yield
                        hbuf_buf.consumed("t0")
                        mpat_buf.consumed("v0")
                    xng_buf.consumed("t0")

            _gen = ffn_gen()
            _done = [False]

            def ffn_step(k=1):
                for _ in range(k):
                    if _done[0]:
                        return
                    if next(_gen, "END") == "END":
                        _done[0] = True
                        return

            wtc_free = [0]
            P.wait_fam("sync", "lw")
            P.wait("vector", "v0", hin_done)
            P.do("vector", lambda e: e.memset(dqcarry[:, :], 0.0), "v0")
            for q in range(NQ):
                qc = q * QT
                for n in range(NST):
                    row = 16 + n
                    bk = 2 + (n % 2)
                    bguard(bk)
                    tq = P.do("tensor", lambda e, row=row, qc=qc, bk=bk: e.matmul(
                        bF(bk), oneh_sb[:, row * 128:(row + 1) * 128],
                        bc_bf[:, qc:qc + 512], start=True, stop=True), "t0")
                    P.wait("vector", "t0", tq)
                    P.do("vector", lambda e, n=n, bk=bk: e.tensor_copy(
                        ccint[:, n:16 * 512:16], bF(bk)), "v0")
                    brelease("v0", bk)
                    ffn_step(2)
                for blk in range(4):
                    y2c_buf.require("vector")
                    if wtc_free[0]:
                        P.wait("sync", "t0", wtc_free[0])
                    lwc = dma("sync", wTc[:, :].rearrange("p (e k) -> p e k", e=4),
                              outT_hbm.ap()[blk * 4:(blk + 1) * 4].rearrange(
                                  "e p k -> p e k"), "la")
                    for ei in range(4):
                        et = blk * 4 + ei
                        bguard(0)
                        P.do("tensor", lambda e, et=et, qc=qc: e.matmul(
                            bF(0), wd2T[:, et * 128:(et + 1) * 128],
                            dlow_bf[:, qc:qc + 512], start=True, stop=True), "t0")
                        tq = P.cnt["t0"]
                        ffn_step(4)
                        P.wait("scalar", "t0", tq)
                        P.wait("scalar", "v0")
                        P.do("scalar", lambda e, et=et: e.activation(
                            silc[:, :], bF(0), AF.Abs, bias=wd2b_sb[:, et:et + 1]), "s0")
                        P.do("scalar", lambda e, et=et: e.activation(
                            dcor[:, :], bF(0), AF.Relu, bias=wd2b_sb[:, et:et + 1]), "s0")
                        brelease("s0", 0)
                        P.do("scalar", lambda e: e.activation(
                            silc[:, :], silc[:, :], AF.Exp, scale=-1.0), "s0")
                        P.do("scalar", lambda e: e.activation(
                            silc[:, :], silc[:, :], AF.Ln, bias=1.0), "s0")
                        P.wait("vector", "s0")
                        P.do("vector", lambda e: e.tensor_add(
                            dcor[:, :], silc[:, :], dcor[:, :]), "v0")
                        init = 0.0 if q == 0 else dqcarry[:, et:et + 1]
                        P.do("vector", lambda e, init=init: e.tensor_tensor_scan(
                            Dbuf[:, :], onesf[:, :], dcor[:, :], init,
                            op0=ALU.mult, op1=ALU.add), "v0")
                        vqD = P.do("vector", lambda e, et=et: e.tensor_copy(
                            dqcarry[:, et:et + 1], Dbuf[:, 511:512]), "v0")
                        ssld_buf.require("sync")
                        sld = dma("sync", ssld[:, :], ss_hbm.ap()[et, :, qc:qc + 512], "la")
                        ffn_step(4)
                        P.wait("scalar", "v0", vqD)
                        P.wait("scalar", "g0")
                        for n in range(NST):
                            P.do("scalar", lambda e, et=et, n=n: e.activation(
                                gbufs[:, n * 512:(n + 1) * 512], Dbuf[:, :], AF.Exp,
                                scale=a_sb[:, et * NST + n:et * NST + n + 1]), "s0")
                        sq_all = P.cnt["s0"]
                        ffn_step(4)
                        P.wait("vector", "s0", sq_all)
                        P.wait("vector", "g0")
                        gtok = {0: 0, 1: 0}
                        for n in range(NST):
                            half = (n // 2) % 2
                            dst = (ycor_v if n == 0 else
                                   ycor_g if n == 1 else
                                   tv if n % 2 == 0 else tg[:, half * 512:(half + 1) * 512])
                            if n >= 3 and n % 2 == 1 and gtok[half]:
                                P.wait("vector", "g0", gtok[half])
                            vq = P.do("vector", lambda e, n=n, et=et, dst=dst:
                                 e.scalar_tensor_tensor(
                                dst[:, :] if dst is not tg else dst,
                                gbufs[:, n * 512:(n + 1) * 512],
                                hin_sb[:, et * NST + n:et * NST + n + 1],
                                ccint[:, n * 512:(n + 1) * 512],
                                op0=ALU.mult, op1=ALU.mult), "v0")
                            if n >= 2 and n % 2 == 0:
                                P.do("vector", lambda e: e.tensor_add(
                                    ycor_v[:, :], ycor_v[:, :], tv[:, :]), "v0")
                            elif n >= 3:
                                P.wait("gpsimd", "v0", vq)
                                gtok[half] = P.do(
                                    "gpsimd", lambda e, half=half: e.tensor_tensor(
                                        ycor_g[:, :], ycor_g[:, :],
                                        tg[:, half * 512:(half + 1) * 512],
                                        op=ALU.add), "g0")
                        P.wait("vector", "g0")
                        P.do("vector", lambda e: e.tensor_add(
                            ycor_v[:, :], ycor_v[:, :], ycor_g[:, :]), "v0")
                        P.waitt("vector", sld)
                        P.do("vector", lambda e, ei=ei: e.tensor_mul(
                            y2c[:, ei * 512:(ei + 1) * 512], ycor_v[:, :], ssld[:, :]), "v0")
                        ssld_buf.consumed("v0")
                        ffn_step(4)
                    P.wait("tensor", "v0")
                    P.waitt("tensor", lwc)
                    for g2c in range(2):
                        bguard(0, 1, 2, 3)
                        for ei in range(4):
                            for dc in range(DC):
                                P.do("tensor", lambda e, ei=ei, dc=dc, g2c=g2c: e.matmul(
                                    PS1[:, dc * 256:(dc + 1) * 256],
                                    wTc[:, ei * 1024 + dc * 128:ei * 1024 + (dc + 1) * 128],
                                    y2c[:, ei * 512 + g2c * 256:ei * 512 + (g2c + 1) * 256],
                                    start=(ei == 0 and dc % 2 == 0),
                                    stop=(ei == 3 and dc % 2 == 1)), "t0")
                        tq = P.cnt["t0"]
                        wtc_free[0] = tq
                        y2c_buf.consumed("t0")
                        P.wait("vector", "t0", tq)
                        stgE_buf.require("vector")
                        vq = P.do("vector", lambda e: e.tensor_copy(
                            stage_e[:, 0:2048], PS1[:, :]), "v0")
                        brelease("v0", 0, 1, 2, 3)
                        P.wait("gpsimd", "v0", vq)
                        sdq = dma_add(("s", q, g2c),
                                      s_hbm.ap()[:, :, qc + g2c * 256:qc + (g2c + 1) * 256]
                                      .rearrange("c p t -> p c t"),
                                      stage_e[:, 0:2048].rearrange(
                                          "p (c t) -> p c t", c=DC),
                                      first=False)
                        stgE_buf.rel = sdq
                        ffn_step(2)
            while not _done[0]:
                ffn_step(8)

        P.barrier()
        if STOP == 5:
            return fin()

        with ExitStack() as ph:
            def sbf(name, shape, dt=F32):
                return ph.enter_context(nc.sbuf_tensor(name, shape, dt))
            accb = sbf("accb", [128, DC * 512])
            tb = sbf("tb", [128, DC * 512])
            sbs = sbf("sbs", [128, DC * 512])
            xnc = sbf("xnc", [128, DC * 512], BF16)
            rtl = sbf("rtl", [128, D])
            outsb = sbf("outsb", [128, D])
            out_buf, rtl_buf = Buf(), Buf()
            tbst_buf = Buf()
            accb_buf, sbs_buf, xnc_buf, tb_buf = Buf(), Buf(), Buf(), Buf()

            P.wait_fam("sync", "sd")
            for g2 in range(NG):
                for th in range(2):
                    t0c = g2 * GT + th * 512
                    accb_buf.require("sync")
                    sbs_buf.require("sync")
                    xnc_buf.require("sync")
                    tb_buf.require("scalar")
                    l1 = dma("sync", accb[:, :].rearrange("p (c t) -> p c t", c=DC),
                             acc_hbm.ap()[g2, :, :, th * 512:(th + 1) * 512].rearrange(
                                 "c p t -> p c t"), "la")
                    l2 = dma("sync", sbs[:, :].rearrange("p (c t) -> p c t", c=DC),
                             s_hbm.ap()[:, :, t0c:t0c + 512].rearrange(
                                 "c p t -> p c t"), "la")
                    l3 = dma("sync", xnc[:, :].rearrange("p (c t) -> p c t", c=DC),
                             xn_hbm.ap()[:, :, t0c:t0c + 512].rearrange(
                                 "c p t -> p c t"), "la")
                    P.waitt("scalar", l1)
                    P.do("scalar", lambda e: e.activation(
                        tb[:, :], accb[:, :], AF.Silu), "s0")
                    P.wait("vector", "s0")
                    P.waitt("vector", l2)
                    P.waitt("vector", l3)
                    P.do("vector", lambda e: e.tensor_add(
                        sbs[:, :], sbs[:, :], xnc[:, :]), "v0")
                    P.do("vector", lambda e: e.tensor_mul(
                        sbs[:, :], sbs[:, :], tb[:, :]), "v0")
                    vq_sg = P.cnt["v0"]
                    accb_buf.consumed("s0")
                    xnc_buf.consumed("v0")
                    tb_buf.consumed("v0")
                    P.wait("tensor", "v0", vq_sg)
                    for tt in range(4):
                        rrow = t0c + tt * 128
                        rtl_buf.require("sync")
                        lr = dma("sync", rtl[:, :],
                                 resid_m.ap()[rrow:rrow + 128, :], "la")
                        for dh in range(2):
                            bguard(2)
                            for dc4 in range(4):
                                dcg = dh * 4 + dc4
                                P.do("tensor", lambda e, dcg=dcg, dc4=dc4, tt=tt: e.matmul(
                                    bF(2)[:, dc4 * 128:(dc4 + 1) * 128],
                                    sbs[:, dcg * 512 + tt * 128:dcg * 512 + (tt + 1) * 128],
                                    idf_sb[:, :], start=(dc4 == 0), stop=(dc4 == 3)), "t0")
                            tq = P.cnt["t0"]
                            P.wait("vector", "t0", tq)
                            P.waitt("vector", lr)
                            out_buf.require("vector")
                            P.do("vector", lambda e, dh=dh: e.tensor_add(
                                outsb[:, dh * 512:(dh + 1) * 512], bF(2),
                                rtl[:, dh * 512:(dh + 1) * 512]), "v0")
                            brelease("v0", 2)
                        rtl_buf.consumed("v0")
                        sbs_buf.consumed("t0")
                        vq = P.cnt["v0"]
                        P.wait("sync", "v0", vq)
                        oq = dma("sync", out_t.ap()[rrow:rrow + 128, :], outsb[:, :], "ob")
                        out_buf.rel = oq

            P.wait_fam("sync", "ob")
            P.wait_fam("gpsimd", "sd")

        block = top.enter_context(nc.Block())
        P.replay(block)

    return nc


_IDENT_F = np.eye(128, dtype=np.float32)
_IDENT_B = np.eye(128, dtype=bfdt)


def _onehot_blocks(nblk, row_of):
    o = np.zeros((128, nblk * 128), dtype=bfdt)
    for j in range(nblk):
        o[row_of(j), j * 128:(j + 1) * 128] = 1.0
    return o


def kernel(L0_attn_resid, L0_attn_norm_w, L0_attn_skip_w, L0_attn_in_w,
           L0_attn_conv_w, L0_attn_conv_b, L0_attn_Wd1, L0_attn_Wd2_w,
           L0_attn_Wd2_b, L0_attn_WB, L0_attn_WC, L0_attn_A_log, L0_attn_W_D,
           L0_attn_out_w, L0_ffn_gate_w, L0_ffn_fc_w, L0_ffn_proj_w):
    resid = np.ascontiguousarray(np.asarray(L0_attn_resid, dtype=np.float32))
    assert resid.shape == (2, 8192, 1024)

    stack_w = np.ascontiguousarray(np.concatenate(
        [np.asarray(L0_attn_Wd1, np.float32),
         np.asarray(L0_attn_WB, np.float32),
         np.asarray(L0_attn_WC, np.float32)], axis=0))
    conv_w2 = np.ascontiguousarray(
        np.asarray(L0_attn_conv_w, np.float32).reshape(E, DCONV))

    common = dict(
        norm_w=np.ascontiguousarray(L0_attn_norm_w, dtype=np.float32),
        in_w=np.ascontiguousarray(L0_attn_in_w, dtype=np.float32),
        skip_w=np.ascontiguousarray(L0_attn_skip_w, dtype=np.float32),
        out_w=np.ascontiguousarray(L0_attn_out_w, dtype=np.float32),
        conv_w=conv_w2,
        conv_b=np.ascontiguousarray(L0_attn_conv_b, dtype=np.float32),
        stack_w=stack_w,
        wd2_w=np.ascontiguousarray(L0_attn_Wd2_w, dtype=np.float32),
        wd2_b=np.ascontiguousarray(L0_attn_Wd2_b, dtype=np.float32),
        a_log=np.ascontiguousarray(L0_attn_A_log, dtype=np.float32),
        w_d=np.ascontiguousarray(L0_attn_W_D, dtype=np.float32),
        gate_w=np.ascontiguousarray(L0_ffn_gate_w, dtype=np.float32),
        fc_w=np.ascontiguousarray(L0_ffn_fc_w, dtype=np.float32),
        proj_w=np.ascontiguousarray(L0_ffn_proj_w, dtype=np.float32),
        ident_f=_IDENT_F, ident_b=_IDENT_B,
        oneh_bf=_onehot_blocks(32, lambda j: 64 + j),
        onem_bf=_onehot_blocks(8, lambda j: j),
    )

    in_maps = []
    for c in range(NCORE):
        sq, ck = divmod(c, 4)
        r0 = ck * T
        rm = np.ascontiguousarray(resid[sq, r0:r0 + T, :])
        rh = np.zeros((128, D), dtype=np.float32)
        if ck > 0:
            rh[125:128, :] = resid[sq, r0 - 3:r0, :]
        ms = np.zeros((128, 24), dtype=np.float32)
        for ti in range(3):
            j = c - 1 - ti
            if ck - 1 - ti >= 0:
                ms[:, ti * 8 + j] = 1.0
        im = dict(common)
        im["resid_m"] = rm
        im["resid_h"] = rh
        im["msel"] = ms
        in_maps.append(im)

    nc = build_program()
    res = run_bass_kernel_spmd(nc, in_maps, core_ids=list(range(NCORE)))

    out = np.empty((2, 8192, 1024), dtype=np.float32)
    for c in range(NCORE):
        sq, ck = divmod(c, 4)
        out[sq, ck * T:(ck + 1) * T, :] = res.results[c]["out"]
    return out


# version 19 (final)
# speedup vs baseline: 1.1306x; 1.1306x over previous
import numpy as np
import ml_dtypes
from contextlib import ExitStack

import concourse.bass as bass
import concourse.mybir as mybir
from concourse.bass_utils import run_bass_kernel_spmd

F32 = mybir.dt.float32
BF16 = mybir.dt.bfloat16
AF = mybir.ActivationFunctionType
ALU = mybir.AluOpType
AX = mybir.AxisListType
bfdt = ml_dtypes.bfloat16

D, E, NST, DDLT, DCONV = 1024, 2048, 16, 64, 4
NEXP, HID = 8, 4096
T = 2048
NCORE = 8
ET, DC, HM = 16, 8, 32
NQ, QT = 4, 512
NG, GT = 2, 1024
EPS = 1e-5
TCOL = 2051
RB = 3

TAPS = False
STOP = 99
ENGS = ("sync", "tensor", "vector", "scalar", "gpsimd")
DMA_FAMS = ("la", "lw", "ob", "sd", "wo")
DMA_K = 4
SEMS = (["t0", "v0", "s0", "g0", "cc"]
        + [f"{f}{j}" for f in DMA_FAMS for j in range(DMA_K)])


class Sched:
    def __init__(self):
        self.q = {e: [] for e in ENGS}
        self.cnt = {s: 0 for s in SEMS}
        self.lastwait = {e: {} for e in ENGS}
        self.fam = {f: {"i": 0, "uses": [0] * DMA_K} for f in DMA_FAMS}
        self._h = None

    def dma_sem_next(self, name):
        f = self.fam[name]
        j = f["i"] % DMA_K
        f["i"] += 1
        key = f"{name}{j}"
        prev = 16 * f["uses"][j]
        f["uses"][j] += 1
        return key, prev, 16 * f["uses"][j]

    def fam_mark(self, name):
        f = self.fam[name]
        return [(f"{name}{j}", 16 * f["uses"][j])
                for j in range(DMA_K) if f["uses"][j]]

    def wait_marks(self, eng, marks):
        for k, v in marks:
            self.wait(eng, k, v)

    def waitt(self, eng, token):
        self.wait(eng, token[0], token[1])

    def wait_fam(self, eng, name):
        self.wait_marks(eng, self.fam_mark(name))

    OWN = {"vector": "v0", "scalar": "s0", "gpsimd": "g0", "tensor": "t0"}

    def do(self, eng, fn, inc=None, k=1):
        if eng in ("vector", "scalar", "gpsimd"):
            own = self.OWN[eng]
            self.wait(eng, own, self.cnt[own])
        if inc is not None:
            self.cnt[inc] += k
        self.q[eng].append((fn, inc, k))
        return self.cnt[inc] if inc is not None else None

    def wait(self, eng, sem, val=None, force=False):
        if val is None:
            val = self.cnt[sem]
        if val <= 0:
            return
        if not force and self.lastwait[eng].get(sem, 0) >= val:
            return
        self.lastwait[eng][sem] = max(val, self.lastwait[eng].get(sem, 0))
        self.q[eng].append((lambda e, s=sem, v=val: e.wait_ge(self._h[s], v), None, 1))

    def barrier(self):
        snap = dict(self.cnt)
        for e in ENGS:
            for s, v in snap.items():
                self.wait(e, s, v)

    def replay(self, block):
        def make(items):
            def run(engobj):
                for fn, inc, k in items:
                    ins = fn(engobj)
                    if inc is not None:
                        ins.then_inc(self._h[inc], k)
            return run
        for name in ENGS:
            getattr(block, name)(make(self.q[name]))


P = Sched()


class Buf:

    def __init__(self):
        self.rel = None

    def require(self, eng):
        if self.rel is not None:
            P.wait(eng, self.rel[0], self.rel[1])

    def consumed(self, sem):
        self.rel = (sem, P.cnt[sem])


def build_program():
    global P
    P = Sched()
    nc = bass.Bass("TRN2", target_bir_lowering=False, debug=False)

    def din(name, shape, dt=F32):
        return nc.dram_tensor(name, shape, dt, kind="ExternalInput")

    resid_m = din("resid_m", [T, D])
    resid_h = din("resid_h", [128, D])
    norm_w = din("norm_w", [D])
    in_w = din("in_w", [E, D])
    skip_w = din("skip_w", [E, D])
    out_w = din("out_w", [D, E])
    conv_w = din("conv_w", [E, DCONV])
    conv_b = din("conv_b", [E])
    stack_w = din("stack_w", [96, E])
    wd2_w = din("wd2_w", [E, DDLT])
    wd2_b = din("wd2_b", [E])
    a_log = din("a_log", [E, NST])
    w_d = din("w_d", [E])
    gate_w = din("gate_w", [D, NEXP])
    fc_w = din("fc_w", [NEXP, D, HID])
    proj_w = din("proj_w", [NEXP, HID, D])
    ident_f = din("ident_f", [128, 128])
    ident_b = din("ident_b", [128, 128], BF16)
    oneh_bf = din("oneh_bf", [128, 32 * 128], BF16)
    onem_bf = din("onem_bf", [128, 8 * 128], BF16)
    msel = din("msel", [128, 24])

    out_t = nc.dram_tensor("out", [T, D], F32, kind="ExternalOutput")

    scratch_kind = dict(kind="ExternalOutput") if TAPS else {}
    xn_hbm = nc.dram_tensor("xn_hbm", [DC, 128, T], BF16, **scratch_kind)
    s_hbm = nc.dram_tensor("s_hbm", [DC, 128, T], F32, **scratch_kind)
    acc_hbm = nc.dram_tensor("acc_hbm", [NG, DC, 128, GT], F32, **scratch_kind)
    ss_hbm = nc.dram_tensor("ss_hbm", [ET, 128, T], BF16, **scratch_kind)
    fc_bf = nc.dram_tensor("fc_bf", [NEXP, D, HID], BF16)
    proj_bf = nc.dram_tensor("proj_bf", [NEXP, HID, D], BF16)
    inT_hbm = nc.dram_tensor("inT_hbm", [ET, 128, 1024], BF16)
    skipT_hbm = nc.dram_tensor("skipT_hbm", [ET, 128, 1024], BF16)
    outT_hbm = nc.dram_tensor("outT_hbm", [ET, 128, 1024], BF16)
    cc_in = nc.dram_tensor("cc_in", [128, 512], F32)
    cc_out = nc.dram_tensor("cc_out", [NCORE * 128, 512], F32)

    bank_free = {}
    ps_halves = {}

    def bguard(*banks):
        for b in banks:
            if b in bank_free:
                P.wait("tensor", *bank_free[b])
            for off in (0, 128):
                if (b, off) in ps_halves:
                    P.wait("tensor", *ps_halves[(b, off)])

    def brelease(sem, *banks):
        for b in banks:
            bank_free[b] = (sem, P.cnt[sem])

    with ExitStack() as top:
        def sball(name, shape, dt=F32):
            return top.enter_context(nc.sbuf_tensor(name, shape, dt))

        PS1 = top.enter_context(nc.psum_tensor("PS1", [128, 2048], F32))
        PS2 = top.enter_context(nc.psum_tensor("PS2", [128, 2048], F32))
        P._h = {}
        for _s in list(P.cnt):
            P._h[_s] = top.enter_context(nc.semaphore(_s))

        def bF(i, n=512):
            t, j = (PS1, i) if i < 4 else (PS2, i - 4)
            return (t[:, j * 512:j * 512 + n])

        def bB(i, n=1024):
            t, j = (PS1, i) if i < 4 else (PS2, i - 4)
            return t[:, :].bitcast(BF16)[:, j * 1024:j * 1024 + n]

        normw_sb = sball("normw_sb", [128, DC])
        convw_sb = sball("convw_sb", [128, ET * DCONV])
        convb_sb = sball("convb_sb", [128, ET])
        wd2b_sb = sball("wd2b_sb", [128, ET])
        wd_sb = sball("wd_sb", [128, ET])
        a_sb = sball("a_sb", [128, ET * NST])
        gw_sb = sball("gw_sb", [128, DC * NEXP])
        idf_sb = sball("idf_sb", [128, 128], F32)
        idb_sb = sball("idb_sb", [128, 128], BF16)
        oneh_sb = sball("oneh_sb", [128, 32 * 128], BF16)
        onem_sb = sball("onem_sb", [128, 8 * 128], BF16)
        msel_sb = sball("msel_sb", [128, 24], F32)
        m_fm = sball("m_fm", [128, T], BF16)
        wd2T = sball("wd2T", [128, ET * 128], BF16)
        dlow_bf = sball("dlow_bf", [128, T], BF16)
        bc_bf = sball("bc_bf", [128, T], BF16)
        dtot = sball("dtot", [128, ET], F32)
        hin_sb = sball("hin_sb", [128, 256], F32)
        lncol = sball("lncol", [128, 12], F32)
        lnsq = sball("lnsq", [128, 1], F32)
        lnact = sball("lnact", [128, 2], F32)
        onesf = sball("onesf", [128, 512], F32)
        mid = ExitStack()
        hcarry = mid.enter_context(nc.sbuf_tensor("hcarry", [128, ET * NST], F32))
        state_sb = mid.enter_context(nc.sbuf_tensor("state_sb", [128, 512], F32))
        stackT = mid.enter_context(nc.sbuf_tensor("stackT", [128, ET * 128], BF16))
        xrms = mid.enter_context(nc.sbuf_tensor("xrms", [128, DC * TCOL], BF16))

        def dma(eng, dst, src, fam, accum=None, slow=False):
            key, prev, tot = P.dma_sem_next(fam)
            if prev:
                P.wait(eng, key, prev)
            if accum is None:
                P.do(eng, lambda e, d=dst, s=src, sl=slow: e.dma_start(
                    out=d, in_=s, allow_slow_non_contiguous=sl), key, 16)
            else:
                P.do(eng, lambda e, d=dst, s=src, a=accum, sl=slow: e.dma_start(
                    out=d, in_=s, accum_op=a, allow_slow_non_contiguous=sl), key, 16)
            return (key, tot)

        sadd_last = {}

        def dma_add(key, dst, src, first):
            if key in sadd_last:
                P.waitt("gpsimd", sadd_last[key])
            v = dma("gpsimd", dst, src, "sd", accum=None if first else ALU.add)
            sadd_last[key] = v
            return v

        tr_pp = [0]

        def tap(name, sbuf_ap, dt):
            if not TAPS:
                return
            tt = nc.dram_tensor("tap_" + name, list(sbuf_ap.shape), dt,
                                kind="ExternalOutput")
            P.wait("sync", "v0")
            P.wait("sync", "s0")
            P.wait("sync", "g0")
            dma("sync", tt.ap(), sbuf_ap, "ob")
            P.barrier()

        def transpose_to(srcslice, bank, dstslice, bf_in=True, scale=None, pp=False):
            if pp:
                off = 128 * (tr_pp[0] % 2)
                tr_pp[0] += 1
            else:
                off = 0
            reg = (bank, off)
            if reg in ps_halves:
                P.wait("tensor", *ps_halves[reg])
            if bank in bank_free:
                P.wait("tensor", *bank_free[bank])
            tq = P.do("tensor", lambda e, s=srcslice, b=bank, o=off, bi=bf_in: e.matmul(
                bF(b, 512)[:, o:o + 128], s, (idb_sb[:, :] if bi else idf_sb[:, :]),
                start=True, stop=True), "t0")
            P.wait("vector", "t0", tq)
            src_ps = bF(bank, 512)[:, off:off + 128]
            if scale is None:
                P.do("vector", lambda e, d=dstslice, s=src_ps: e.tensor_copy(d, s), "v0")
            else:
                P.do("vector", lambda e, d=dstslice, s=src_ps, sc=scale:
                     e.tensor_scalar_mul(d, s, sc), "v0")
            ps_halves[reg] = ("v0", P.cnt["v0"])

        def fin():
            blk = top.enter_context(nc.Block())
            P.replay(blk)
            return nc

        dma("sync", normw_sb[:, :], norm_w.ap().rearrange("(c p) -> p c", p=128), "la", slow=True)
        dma("sync", convw_sb[:, :].rearrange("p (c k) -> p c k", c=ET),
    conv_w.ap().rearrange("(c p) k -> p c k", p=128), "la")
        dma("sync", convb_sb[:, :], conv_b.ap().rearrange("(c p) -> p c", p=128), "la", slow=True)
        dma("sync", wd2b_sb[:, :], wd2_b.ap().rearrange("(c p) -> p c", p=128), "la", slow=True)
        dma("sync", wd_sb[:, :], w_d.ap().rearrange("(c p) -> p c", p=128), "la", slow=True)
        dma("sync", a_sb[:, :].rearrange("p (c n) -> p c n", c=ET),
    a_log.ap().rearrange("(c p) n -> p c n", p=128), "la")
        dma("sync", gw_sb[:, :].rearrange("p (c g) -> p c g", c=DC),
    gate_w.ap().rearrange("(c p) g -> p c g", p=128), "la")
        dma("sync", idf_sb[:, :], ident_f.ap(), "la")
        dma("sync", idb_sb[:, :], ident_b.ap(), "la")
        dma("sync", oneh_sb[:, :], oneh_bf.ap(), "la")
        dma("sync", onem_sb[:, :], onem_bf.ap(), "la")
        dma("sync", msel_sb[:, :], msel.ap(), "la")
        la_consts = P.fam_mark("la")

        for buf, val in [(m_fm[:, :], 0.0), (dlow_bf[:, :], 0.0), (bc_bf[:, :], 0.0),
                         (onesf[:, :], 1.0)]:
            P.do("vector", lambda e, b=buf, v=val: e.memset(b, v), "v0")

        P.wait_marks("scalar", la_consts)
        P.do("scalar", lambda e: e.activation(a_sb[:, :], a_sb[:, :], AF.Exp), "s0")
        P.wait("vector", "s0")
        P.do("vector", lambda e: e.tensor_scalar_mul(a_sb[:, :], a_sb[:, :], -1.0), "v0")

        with ExitStack() as ph:
            def sb0(name, shape, dt=F32):
                return ph.enter_context(nc.sbuf_tensor(name, shape, dt))
            rtile = sb0("rtile", [128, D])
            sqsc = sb0("sqsc", [128, D])
            xn_tm = sb0("xn_tm", [128, D])
            xr_tm = sb0("xr_tm", [128, D], BF16)
            xnf_st = sb0("xnf_st", [128, 128])
            xn_st = sb0("xn_st", [128, D], BF16)
            z8 = sb0("z8", [128, 8])
            zc = sb0("zc", [128, 3 * 16])
            eqs = sb0("eqs", [128, 2 * 8 * 16])
            w12 = sb0("w12", [128, 3 * 16])
            mpad = sb0("mpad", [128, 128])
            t2 = sb0("t2", [128, 8])
            stk_raw = sb0("stk_raw", [128, E], BF16)
            wd2_raw = sb0("wd2_raw", [128, ET * 128], BF16)
            wstg = sb0("wstg", [128, 2 * 1024], BF16)
            wout = sb0("wout", [128, 2 * 1024], BF16)

            P.do("vector", lambda e: e.memset(mpad[:, :], 0.0), "v0")
            P.do("vector", lambda e: e.memset(stk_raw[96:128, :], 0.0), "v0")
            vq_ms = P.do("vector", lambda e: e.memset(wd2_raw[:, :], 0.0), "v0")

            dma("gpsimd", stk_raw[0:96, :], stack_w.ap(), "lw")
            P.wait("gpsimd", "v0", vq_ms)
            dma("gpsimd", wd2_raw[:, :].rearrange("p (c k) -> p c k", c=ET)[:, :, 0:DDLT],
                wd2_w.ap().rearrange("(c p) k -> p c k", p=128), "lw")
            lw_w = P.fam_mark("lw")

            P.wait_marks("tensor", lw_w)
            P.wait_marks("tensor", la_consts)
            P.wait("tensor", "v0", vq_ms)
            for srcb, dstb in [(stk_raw, stackT), (wd2_raw, wd2T)]:
                for et in range(ET):
                    transpose_to(srcb[:, et * 128:(et + 1) * 128], 7,
                                 dstb[:, et * 128:(et + 1) * 128])

            wjobs = []
            for et in range(ET):
                wjobs.append((in_w.ap()[et * 128:(et + 1) * 128, :], inT_hbm, et))
            for et in range(ET):
                wjobs.append((skip_w.ap()[et * 128:(et + 1) * 128, :], skipT_hbm, et))
            for et in range(ET):
                wjobs.append((out_w.ap()[:, et * 128:(et + 1) * 128].rearrange(
                    "(c p) q -> p c q", p=128), outT_hbm, et))
            wj_state = {"k": 0, "in": [0, 0], "out": [0, 0]}

            def wjob_load(k):
                h = k % 2
                src = wjobs[k][0]
                if wj_state["out"][h]:
                    P.wait("gpsimd", "t0", wj_state["in"][h])
                dst = wstg[:, h * 1024:(h + 1) * 1024]
                if len(src.shape) == 3:
                    dst = dst.rearrange("p (c q) -> p c q", c=DC)
                return dma("gpsimd", dst, src, "lw")

            def wjob_step():
                k = wj_state["k"]
                if k >= len(wjobs):
                    return
                wj_state["k"] = k + 1
                if k == 0:
                    wjob_load(0)
                h = k % 2
                P.wait_fam("tensor", "lw")
                if k + 1 < len(wjobs):
                    wjob_load(k + 1)
                _, dsth, et = wjobs[k]
                if wj_state["out"][h]:
                    P.waitt("scalar", wj_state["out"][h])
                for j in range(8):
                    bk = 3 if j % 2 == 0 else 6
                    bguard(bk)
                    tq = P.do("tensor", lambda e, j=j, h=h, bk=bk: e.matmul(
                        bF(bk, 128), wstg[:, h * 1024 + j * 128:h * 1024 + (j + 1) * 128],
                        idb_sb[:, :], start=True, stop=True), "t0")
                    P.wait("scalar", "t0", tq)
                    P.do("scalar", lambda e, j=j, h=h, bk=bk: e.activation(
                        wout[:, h * 1024 + j * 128:h * 1024 + (j + 1) * 128],
                        bF(bk, 128), AF.Copy), "s0")
                    brelease("s0", bk)
                wj_state["in"][h] = P.cnt["t0"]
                sq = P.cnt["s0"]
                P.wait("gpsimd", "s0", sq)
                oq = dma("gpsimd", dsth.ap()[et], wout[:, h * 1024:(h + 1) * 1024], "wo")
                wj_state["out"][h] = oq

            xnst_buf = Buf()
            rtile_buf = Buf()
            for ti in range(17):
                real = ti > 0
                i = ti - 1
                for _ in range(3):
                    wjob_step()
                rtile_buf.require("sync")
                if ti == 0:
                    lq = dma("sync", rtile[:, :], resid_h.ap(), "la")
                else:
                    lq = dma("sync", rtile[:, :], resid_m.ap()[i * 128:(i + 1) * 128, :], "la")
                P.waitt("vector", lq)
                P.waitt("scalar", lq)
                P.do("vector", lambda e: e.reduce_sum(
                    lncol[:, 0:1], rtile[:, :], axis=AX.X), "v0")
                P.wait("scalar", "v0")
                P.do("scalar", lambda e: e.activation(
                    sqsc[:, :], rtile[:, :], AF.Square, accum_out=lnsq[:, 0:1]), "s0")
                P.wait("vector", "s0")
                P.do("vector", lambda e: e.tensor_scalar_mul(
                    lncol[:, 2:3], lncol[:, 0:1], 1.0 / D), "v0")
                P.do("vector", lambda e: e.tensor_scalar_mul(
                    lncol[:, 3:4], lnsq[:, 0:1], 1.0 / D), "v0")
                P.wait("vector", "v0", force=True)
                P.do("vector", lambda e: e.tensor_mul(
                    lncol[:, 4:5], lncol[:, 2:3], lncol[:, 2:3]), "v0")
                P.do("vector", lambda e: e.tensor_sub(
                    lncol[:, 4:5], lncol[:, 3:4], lncol[:, 4:5]), "v0")
                P.do("vector", lambda e: e.tensor_scalar_add(
                    lncol[:, 5:6], lncol[:, 4:5], EPS), "v0")
                P.do("vector", lambda e: e.reciprocal(
                    lncol[:, 6:7], lncol[:, 5:6]), "v0")
                P.do("vector", lambda e: e.tensor_mul(
                    lncol[:, 7:8], lncol[:, 4:5], lncol[:, 6:7]), "v0")
                P.do("vector", lambda e: e.tensor_scalar_add(
                    lncol[:, 7:8], lncol[:, 7:8], EPS), "v0")
                P.wait("scalar", "v0")
                P.do("scalar", lambda e: e.activation(
                    lnact[:, 0:1], lncol[:, 5:6], AF.Ln), "s0")
                P.do("scalar", lambda e: e.activation(
                    lnact[:, 0:1], lnact[:, 0:1], AF.Exp, scale=-0.5), "s0")
                P.do("scalar", lambda e: e.activation(
                    lnact[:, 1:2], lncol[:, 7:8], AF.Ln), "s0")
                P.do("scalar", lambda e: e.activation(
                    lnact[:, 1:2], lnact[:, 1:2], AF.Exp, scale=-0.5), "s0")
                P.wait("vector", "s0")
                P.do("vector", lambda e: e.tensor_mul(
                    lncol[:, 10:11], lncol[:, 2:3], lnact[:, 0:1]), "v0")
                P.do("vector", lambda e: e.tensor_scalar_mul(
                    lncol[:, 10:11], lncol[:, 10:11], -1.0), "v0")
                P.do("vector", lambda e: e.tensor_mul(
                    lncol[:, 11:12], lnact[:, 0:1], lnact[:, 1:2]), "v0")
                P.do("vector", lambda e: e.tensor_mul(
                    lncol[:, 0:1], lncol[:, 2:3], lncol[:, 11:12]), "v0")
                P.do("vector", lambda e: e.tensor_scalar_mul(
                    lncol[:, 0:1], lncol[:, 0:1], -1.0), "v0")
                if ti == 1:
                    tap("rtile1", rtile[:, :], F32)
                    tap("lncol1", lncol[:, :], F32)
                P.wait("vector", "v0", force=True)
                P.do("vector", lambda e: e.tensor_scalar(
                    xn_tm[:, :], rtile[:, :], lnact[:, 0:1], lncol[:, 10:11],
                    op0=ALU.mult, op1=ALU.add), "v0")
                vq = P.do("vector", lambda e: e.tensor_scalar(
                    xr_tm[:, :], rtile[:, :], lncol[:, 11:12], lncol[:, 0:1],
                    op0=ALU.mult, op1=ALU.add), "v0")
                rtile_buf.consumed("v0")
                if ti == 1:
                    tap("xntm1", xn_tm[:, :], F32)
                    tap("xrtm1", xr_tm[:, :], BF16)
                P.wait("tensor", "v0", vq)
                xnst_buf.require("vector")
                for dc in range(DC):
                    bguard(2)
                    tq = P.do("tensor", lambda e, dc=dc: e.matmul(
                        bF(2, 128), xn_tm[:, dc * 128:(dc + 1) * 128],
                        idf_sb[:, :], start=True, stop=True), "t0")
                    P.wait("vector", "t0", tq)
                    if real:
                        P.do("vector", lambda e: e.tensor_copy(
                            xnf_st[:, :], bF(2, 128)), "v0")
                    P.do("vector", lambda e, dc=dc: e.tensor_copy(
                        xn_st[:, dc * 128:(dc + 1) * 128], bF(2, 128)), "v0")
                    brelease("v0", 2)
                    if real:
                        P.wait("tensor", "v0")
                        if dc == 0:
                            bguard(0)
                        P.do("tensor", lambda e, dc=dc: e.matmul(
                            bF(0, 8), xnf_st[:, :], gw_sb[:, dc * 8:(dc + 1) * 8],
                            start=(dc == 0), stop=(dc == DC - 1)), "t0")
                tq_z = P.cnt["t0"]
                for dc in range(DC):
                    bguard(7)
                    tq = P.do("tensor", lambda e, dc=dc: e.matmul(
                        bF(7, 128), xr_tm[:, dc * 128:(dc + 1) * 128],
                        idb_sb[:, :], start=True, stop=True), "t0")
                    P.wait("vector", "t0", tq)
                    if ti == 0:
                        P.do("vector", lambda e, dc=dc: e.tensor_scalar_mul(
                            xrms[:, dc * TCOL:dc * TCOL + 3],
                            bF(7, 128)[:, 125:128], normw_sb[:, dc:dc + 1]), "v0")
                    else:
                        cb = RB + i * 128
                        P.do("vector", lambda e, dc=dc, cb=cb: e.tensor_scalar_mul(
                            xrms[:, dc * TCOL + cb:dc * TCOL + cb + 128],
                            bF(7, 128), normw_sb[:, dc:dc + 1]), "v0")
                    brelease("v0", 7)
                if real:
                    P.wait("sync", "v0")
                    xq = dma("sync",
                             xn_hbm.ap()[:, :, i * 128:(i + 1) * 128].rearrange(
                                 "c p t -> p c t"),
                             xn_st[:, :].rearrange("p (c t) -> p c t", c=DC), "ob")
                    xnst_buf.rel = xq
                    P.wait("vector", "t0", tq_z)
                    P.do("vector", lambda e: e.tensor_copy(z8[:, :], bF(0, 8)), "v0")
                    brelease("v0", 0)
                    c0, e1c = 3 * i, 16 * i
                    P.do("vector", lambda e, c0=c0: e.reduce_max(
                        zc[:, c0:c0 + 1], z8[:, :], axis=AX.X), "v0")
                    P.wait("vector", "v0", force=True)
                    P.do("vector", lambda e, c0=c0, e1c=e1c: e.tensor_single_scalar(
                        eqs[:, e1c:e1c + 8], z8[:, :], zc[:, c0:c0 + 1],
                        op=ALU.is_equal), "v0")
                    P.do("vector", lambda e, e1c=e1c: e.scalar_tensor_tensor(
                        z8[:, :], eqs[:, e1c:e1c + 8], -1e30, z8[:, :],
                        op0=ALU.mult, op1=ALU.add), "v0")
                    P.do("vector", lambda e, c0=c0: e.reduce_max(
                        zc[:, c0 + 1:c0 + 2], z8[:, :], axis=AX.X), "v0")
                    P.wait("vector", "v0", force=True)
                    P.do("vector", lambda e, c0=c0, e1c=e1c: e.tensor_single_scalar(
                        eqs[:, e1c + 8:e1c + 16], z8[:, :], zc[:, c0 + 1:c0 + 2],
                        op=ALU.is_equal), "v0")
                    P.do("vector", lambda e, c0=c0: e.tensor_sub(
                        zc[:, c0 + 2:c0 + 3], zc[:, c0 + 1:c0 + 2], zc[:, c0:c0 + 1]), "v0")

            while wj_state["k"] < len(wjobs):
                wjob_step()
            P.wait_fam("gpsimd", "wo")

            P.wait("scalar", "v0")
            P.do("scalar", lambda e: e.activation(
                w12[:, 0:16], zc[:, 2:48:3], AF.Exp), "s0")
            P.wait("vector", "s0")
            P.do("vector", lambda e: e.tensor_scalar_add(
                w12[:, 0:16], w12[:, 0:16], 1.0), "v0")
            P.do("vector", lambda e: e.reciprocal(w12[:, 16:32], w12[:, 0:16]), "v0")
            P.do("vector", lambda e: e.tensor_scalar(
                w12[:, 32:48], w12[:, 16:32], -1.0, 1.0,
                op0=ALU.mult, op1=ALU.add), "v0")
            P.wait("vector", "v0", force=True)
            for i in range(16):
                e1c = 16 * i
                P.do("vector", lambda e, e1c=e1c: e.tensor_mul(
                    t2[:, :], eqs[:, e1c + 8:e1c + 16], eqs[:, e1c:e1c + 8]), "v0")
                P.do("vector", lambda e, e1c=e1c: e.tensor_sub(
                    t2[:, :], eqs[:, e1c + 8:e1c + 16], t2[:, :]), "v0")
                P.do("vector", lambda e, i=i: e.tensor_scalar_mul(
                    t2[:, :], t2[:, :], w12[:, 32 + i:33 + i]), "v0")
                vq = P.do("vector", lambda e, e1c=e1c, i=i: e.scalar_tensor_tensor(
                    mpad[:, 0:8], eqs[:, e1c:e1c + 8], w12[:, 16 + i:17 + i], t2[:, :],
                    op0=ALU.mult, op1=ALU.add), "v0")
                P.wait("tensor", "v0", vq)
                bguard(1)
                tq = P.do("tensor", lambda e: e.matmul(
                    bF(1, 128), mpad[:, :], idf_sb[:, :], start=True, stop=True), "t0")
                P.wait("vector", "t0", tq)
                P.do("vector", lambda e, i=i: e.tensor_copy(
                    m_fm[0:8, i * 128:(i + 1) * 128], bF(1, 128)[0:8, :]), "v0")
                brelease("v0", 1)

        tap("xrms", xrms[:, :], BF16)
        tap("mfm", m_fm[:, :], BF16)
        P.barrier()
        if STOP == 1:
            return fin()

        with ExitStack() as ph:
            def sbq(name, shape, dt=F32):
                return ph.enter_context(nc.sbuf_tensor(name, shape, dt))
            x_all = sbq("x_all", [128, ET * QT], BF16)
            d2 = sbq("d2", [128, 2 * QT], BF16)
            dsc_e = sbq("dsc_e", [128, QT])
            dsc_f = sbq("dsc_f", [128, QT])
            y_all = sbq("y_all", [128, ET * QT], BF16)
            bint = sbq("bint", [128, 16 * QT], BF16)
            cint = sbq("cint", [128, 16 * QT], BF16)
            hint = sbq("hint", [128, 16 * QT], BF16)
            wTq = sbq("wTq", [128, 2 * 1024], BF16)
            skT = sbq("skT", [128, 4 * 1024], BF16)
            wT2b = sbq("wT2b", [128, 4 * 1024], BF16)
            x_in = sbq("x_in", [128, 2 * 515])
            sil_e = sbq("sil_e", [128, 2 * 512])
            sil_f = sbq("sil_f", [128, 512])
            abar = sbq("abar", [128, 3 * 512], BF16)
            ubuf = sbq("ubuf", [128, 3 * 512], BF16)
            dxb = sbq("dxb", [128, 2 * 512], BF16)
            ss_et = sbq("ss_et", [128, 512], BF16)
            y2_et = sbq("y2_et", [128, 4 * 512], BF16)
            stage_s = sbq("stage_s", [128, 2048])

            sset_buf, stage_buf, y2_buf = Buf(), Buf(), Buf()
            pc_jobs = []
            for ex in range(NEXP):
                for i in range(8):
                    pc_jobs.append((fc_w.ap()[ex, i * 128:(i + 1) * 128, :],
                                    fc_bf.ap()[ex, i * 128:(i + 1) * 128, :]))
                    pc_jobs.append((proj_w.ap()[ex, i * 512:(i + 1) * 512, :],
                                    proj_bf.ap()[ex, i * 512:(i + 1) * 512, :]))
            pc_k = [0]

            def pc_step(n=1):
                for _ in range(n):
                    if pc_k[0] < len(pc_jobs):
                        src, dst = pc_jobs[pc_k[0]]
                        pc_k[0] += 1
                        dma("gpsimd", dst, src, "lw")
            wq_tok = [0, 0]
            wq_free = [0, 0]

            def wq_load(et):
                h = et % 2
                if wq_free[h]:
                    P.wait("sync", "t0", wq_free[h])
                wq_tok[h] = dma("sync", wTq[:, h * 1024:(h + 1) * 1024],
                                inT_hbm.ap()[et], "la")

            for q in range(NQ):
                base = RB + q * QT - 3
                qc = q * QT
                def stage1_mm(k, base=base):
                    h = k % 2
                    bp = (4, 5) if h == 0 else (2, 3)
                    P.waitt("tensor", wq_tok[h])
                    for (c0, cn, bk) in [(base, 512, bp[0]), (base + 451, 64, bp[1])]:
                        bguard(bk)
                        for kc in range(8):
                            P.do("tensor", lambda e, kc=kc, c0=c0, cn=cn, bk=bk,
                                 h=h: e.matmul(
                                bF(bk, cn),
                                wTq[:, h * 1024 + kc * 128:h * 1024 + (kc + 1) * 128],
                                xrms[:, kc * TCOL + c0:kc * TCOL + c0 + cn],
                                start=(kc == 0), stop=(kc == 7)), "t0")
                    mm_tok[h] = P.cnt["t0"]
                    wq_free[h] = P.cnt["t0"]

                def stage1_evac(k):
                    h = k % 2
                    bp = (4, 5) if h == 0 else (2, 3)
                    P.wait("vector", "t0", mm_tok[h])
                    for (cn, xo, bk) in [(512, 0, bp[0]), (64, 451, bp[1])]:
                        P.do("vector", lambda e, cn=cn, xo=xo, bk=bk, h=h: e.tensor_copy(
                            x_in[:, h * 515 + xo:h * 515 + xo + cn], bF(bk, cn)), "v0")
                        brelease("v0", bk)

                wq_load(0)
                wq_load(1)
                mm_tok = [0, 0]
                sil_tok = [0, 0]
                stage1_mm(0)
                stage1_evac(0)
                for et in range(ET):
                    h = et % 2
                    pc_step(1)
                    if et + 2 < ET:
                        wq_load(et + 2)
                    if et + 1 < ET:
                        stage1_mm(et + 1)
                    P.wait("vector", "s0", sil_tok[h])
                    P.do("vector", lambda e, et=et, h=h: e.tensor_scalar_mul(
                        sil_e[:, h * 512:h * 512 + 512], x_in[:, h * 515:h * 515 + 512],
                        convw_sb[:, et * DCONV:et * DCONV + 1]), "v0")
                    for j in range(1, DCONV):
                        P.do("vector", lambda e, j=j, et=et, h=h: e.scalar_tensor_tensor(
                            sil_e[:, h * 512:h * 512 + 512],
                            x_in[:, h * 515 + j:h * 515 + j + 512],
                            convw_sb[:, et * DCONV + j:et * DCONV + j + 1],
                            sil_e[:, h * 512:h * 512 + 512],
                            op0=ALU.mult, op1=ALU.add), "v0")
                    if et + 1 < ET:
                        stage1_evac(et + 1)
                    P.wait("scalar", "v0")
                    sq = P.do("scalar", lambda e, et=et, h=h: e.activation(
                        x_all[:, et * QT:(et + 1) * QT],
                        sil_e[:, h * 512:h * 512 + 512], AF.Silu,
                        bias=convb_sb[:, et:et + 1]), "s0")
                    sil_tok[h] = sq
                P.wait("tensor", "s0")
                bguard(6)
                for et in range(ET):
                    P.do("tensor", lambda e, et=et: e.matmul(
                        bF(6), stackT[:, et * 128:(et + 1) * 128],
                        x_all[:, et * QT:(et + 1) * QT],
                        start=(et == 0), stop=(et == ET - 1)), "t0")
                tq = P.cnt["t0"]
                P.wait("vector", "t0", tq)
                P.do("vector", lambda e, qc=qc: e.tensor_copy(
                    dlow_bf[0:64, qc:qc + 512], bF(6)[0:64, :]), "v0")
                P.do("vector", lambda e, qc=qc: e.tensor_copy(
                    bc_bf[64:96, qc:qc + 512], bF(6)[64:96, :]), "v0")
                brelease("v0", 6)
                P.wait("tensor", "v0")

                for bci in range(2):
                    for n in range(NST):
                        row = 16 * bci + n
                        bk = 6 + (n % 2)
                        bguard(bk)
                        tq = P.do("tensor", lambda e, row=row, qc=qc, bk=bk: e.matmul(
                            bF(bk), oneh_sb[:, row * 128:(row + 1) * 128],
                            bc_bf[:, qc:qc + 512], start=True, stop=True), "t0")
                        P.wait("scalar", "t0", tq)
                        if bci == 0:
                            P.do("scalar", lambda e, n=n, bk=bk: e.activation(
                                bint[:, n * QT:(n + 1) * QT], bF(bk), AF.Copy), "s0")
                        else:
                            P.do("scalar", lambda e, n=n, bk=bk: e.activation(
                                cint[:, n:NST * QT:NST], bF(bk), AF.Copy), "s0")
                        brelease("s0", bk)

                sq_bc = P.cnt["s0"]
                P.wait("vector", "s0", sq_bc)
                P.wait("gpsimd", "s0", sq_bc)
                d2_wr = [0, 0]
                d2_rd = [0, 0]

                def dstage(k, q=q, qc=qc):
                    h = k % 2
                    bguard(0)
                    P.do("tensor", lambda e, k=k, qc=qc: e.matmul(
                        bF(0), wd2T[:, k * 128:(k + 1) * 128],
                        dlow_bf[:, qc:qc + 512], start=True, stop=True), "t0")
                    tqd = P.cnt["t0"]
                    P.wait("scalar", "t0", tqd)
                    P.wait("scalar", "v0")
                    P.do("scalar", lambda e, k=k: e.activation(
                        dsc_e[:, :], bF(0), AF.Abs, bias=wd2b_sb[:, k:k + 1]), "s0")
                    P.do("scalar", lambda e, k=k: e.activation(
                        dsc_f[:, :], bF(0), AF.Relu, bias=wd2b_sb[:, k:k + 1]), "s0")
                    brelease("s0", 0)
                    P.do("scalar", lambda e: e.activation(
                        dsc_e[:, :], dsc_e[:, :], AF.Exp, scale=-1.0), "s0")
                    sqd = P.do("scalar", lambda e: e.activation(
                        dsc_e[:, :], dsc_e[:, :], AF.Ln, bias=1.0), "s0")
                    if d2_rd[h]:
                        P.wait("vector", "s0", max(sqd, d2_rd[h]))
                    else:
                        P.wait("vector", "s0", sqd)
                    P.do("vector", lambda e, h=h: e.tensor_add(
                        d2[:, h * QT:(h + 1) * QT], dsc_e[:, :], dsc_f[:, :]), "v0")
                    P.do("vector", lambda e, h=h: e.reduce_sum(
                        lncol[:, 0:1], d2[:, h * QT:(h + 1) * QT], axis=AX.X), "v0")
                    if q == 0:
                        P.do("vector", lambda e, k=k: e.tensor_copy(
                            dtot[:, k:k + 1], lncol[:, 0:1]), "v0")
                    else:
                        P.do("vector", lambda e, k=k: e.tensor_add(
                            dtot[:, k:k + 1], dtot[:, k:k + 1], lncol[:, 0:1]), "v0")
                    d2_wr[h] = P.cnt["v0"]

                def dxb_mul(k):
                    h2 = k % 2
                    if ub_tok[h2]:
                        P.wait("vector", "g0", ub_tok[h2])
                    vq2[h2] = P.do("vector", lambda e, k=k, h2=h2: e.tensor_mul(
                        dxb[:, h2 * 512:(h2 + 1) * 512], d2[:, h2 * QT:(h2 + 1) * QT],
                        x_all[:, k * QT:(k + 1) * QT]), "v0")

                vq2 = [0, 0]
                ub_tok = [0, 0]
                scan_done = {j: 0 for j in range(3)}
                dstage(0)
                dxb_mul(0)
                for et in range(ET):
                    h = et % 2
                    P.wait("gpsimd", "v0", vq2[h])
                    P.wait("scalar", "v0", d2_wr[h])
                    for n in range(NST):
                        if n == 8 and et + 1 < ET:
                            dstage(et + 1)
                        if n == 12 and et + 1 < ET:
                            dxb_mul(et + 1)
                        pp = (et * NST + n) % 3
                        if scan_done[pp]:
                            P.wait("scalar", "v0", scan_done[pp])
                            P.wait("gpsimd", "v0", scan_done[pp])
                        sq = P.do("scalar", lambda e, et=et, n=n, pp=pp, h=h: e.activation(
                            abar[:, pp * 512:(pp + 1) * 512],
                            d2[:, h * QT:(h + 1) * QT], AF.Exp,
                            scale=a_sb[:, et * NST + n:et * NST + n + 1]), "s0")
                        gq = P.do("gpsimd", lambda e, n=n, pp=pp, h=h: e.tensor_tensor(
                            ubuf[:, pp * 512:(pp + 1) * 512],
                            dxb[:, h * 512:(h + 1) * 512],
                            bint[:, n * QT:(n + 1) * QT], op=ALU.mult), "g0")
                        if n == NST - 1:
                            ub_tok[h] = gq
                        P.wait("vector", "s0", sq)
                        P.wait("vector", "g0", gq)
                        init = 0.0 if q == 0 else hcarry[:, et * NST + n:et * NST + n + 1]
                        scan_done[pp] = P.do(
                            "vector", lambda e, n=n, pp=pp, init=init: e.tensor_tensor_scan(
                                hint[:, n:NST * QT:NST],
                                abar[:, pp * 512:(pp + 1) * 512],
                                ubuf[:, pp * 512:(pp + 1) * 512],
                                init, op0=ALU.mult, op1=ALU.add), "v0")
                    d2_rd[h] = P.cnt["s0"]
                    P.do("vector", lambda e, et=et: e.tensor_copy(
                        hcarry[:, et * NST:(et + 1) * NST],
                        hint[:, (QT - 1) * NST:QT * NST]), "v0")
                    P.do("vector", lambda e: e.tensor_mul(
                        hint[:, :], hint[:, :], cint[:, :]), "v0")
                    P.wait("vector", "s0")
                    vqr = P.do("vector", lambda e: e.tensor_reduce(
                        sil_f[:, :],
                        hint[:, :].rearrange("p (l n) -> p l n", n=16),
                        axis=AX.X, op=ALU.add), "v0")
                    P.wait("scalar", "v0", vqr)
                    P.do("scalar", lambda e, et=et: e.activation(
                        y_all[:, et * QT:(et + 1) * QT], sil_f[:, :], AF.Copy), "s0")

                if q == 0:
                    tap("xall0", x_all[:, :], BF16)
                    tap("yall0", y_all[:, :], BF16)
                for blk in range(4):
                    y2_buf.require("vector")
                    P.wait("sync", "t0")
                    lsk = dma("sync", skT[:, :].rearrange("p (e k) -> p e k", e=4),
                              skipT_hbm.ap()[blk * 4:(blk + 1) * 4].rearrange(
                                  "e p k -> p e k"), "la")
                    lot = dma("sync", wT2b[:, :].rearrange("p (e k) -> p e k", e=4),
                              outT_hbm.ap()[blk * 4:(blk + 1) * 4].rearrange(
                                  "e p k -> p e k"), "la")
                    P.waitt("tensor", lsk)
                    P.waitt("tensor", lot)
                    for ei in range(4):
                        et = blk * 4 + ei
                        pc_step(1)
                        bguard(4)
                        for kc in range(8):
                            P.do("tensor", lambda e, kc=kc, q=q, ei=ei: e.matmul(
                                bF(4), skT[:, ei * 1024 + kc * 128:
                                           ei * 1024 + (kc + 1) * 128],
                                xrms[:, kc * TCOL + RB + q * QT:
                                     kc * TCOL + RB + q * QT + 512],
                                start=(kc == 0), stop=(kc == 7)), "t0")
                        tq = P.cnt["t0"]
                        P.wait("scalar", "t0", tq)
                        sset_buf.require("scalar")
                        P.wait("scalar", "v0")
                        sq = P.do("scalar", lambda e: e.activation(
                            ss_et[:, :], bF(4), AF.Silu), "s0")
                        brelease("s0", 4)
                        P.wait("sync", "s0", sq)
                        sq2 = dma("sync", ss_hbm.ap()[et, :, qc:qc + 512], ss_et[:, :], "ob")
                        sset_buf.rel = sq2
                        P.wait("vector", "s0", sq)
                        P.do("vector", lambda e, et=et: e.scalar_tensor_tensor(
                            dxb[:, 0:512], x_all[:, et * QT:(et + 1) * QT],
                            wd_sb[:, et:et + 1], y_all[:, et * QT:(et + 1) * QT],
                            op0=ALU.mult, op1=ALU.add), "v0")
                        P.do("vector", lambda e, ei=ei: e.tensor_mul(
                            y2_et[:, ei * 512:(ei + 1) * 512], dxb[:, 0:512],
                            ss_et[:, :]), "v0")
                    P.wait("tensor", "v0")
                    for g2c in range(2):
                        bguard(0, 1, 2, 3)
                        for ei in range(4):
                            for dc in range(DC):
                                P.do("tensor", lambda e, ei=ei, dc=dc, g2c=g2c: e.matmul(
                                    PS1[:, dc * 256:(dc + 1) * 256],
                                    wT2b[:, ei * 1024 + dc * 128:ei * 1024 + (dc + 1) * 128],
                                    y2_et[:, ei * 512 + g2c * 256:ei * 512 + (g2c + 1) * 256],
                                    start=(ei == 0 and dc % 2 == 0),
                                    stop=(ei == 3 and dc % 2 == 1)), "t0")
                        tq = P.cnt["t0"]
                        y2_buf.consumed("t0")
                        P.wait("scalar", "t0", tq)
                        stage_buf.require("scalar")
                        vq = P.do("scalar", lambda e: e.activation(
                            stage_s[:, :], PS1[:, :], AF.Copy), "s0")
                        brelease("s0", 0, 1, 2, 3)
                        P.wait("gpsimd", "s0", vq)
                        sdq = dma_add(("s", q, g2c),
                                      s_hbm.ap()[:, :, qc + g2c * 256:qc + (g2c + 1) * 256]
                                      .rearrange("c p t -> p c t"),
                                      stage_s[:, :].rearrange("p (c t) -> p c t", c=DC),
                                      first=(blk == 0))
                        stage_buf.rel = sdq
                P.barrier()

            pc_step(len(pc_jobs))
            P.wait("scalar", "v0")
            for et in range(ET):
                P.do("scalar", lambda e, et=et: e.activation(
                    state_sb[:, et * NST:(et + 1) * NST],
                    a_sb[:, et * NST:(et + 1) * NST], AF.Exp,
                    scale=dtot[:, et:et + 1]), "s0")
            P.wait("vector", "s0")
            P.do("vector", lambda e: e.tensor_copy(
                state_sb[:, 256:512], hcarry[:, :]), "v0")
            vq = P.cnt["v0"]
            P.wait("gpsimd", "v0", vq)
            dma("gpsimd", cc_in.ap(), state_sb[:, :], "lw")
            P.wait_fam("gpsimd", "lw")
            P.do("gpsimd", lambda e: e.collective_compute(
                "AllGather", ALU.bypass,
                replica_groups=[list(range(NCORE))],
                ins=[cc_in.ap().opt()], outs=[cc_out.ap().opt()]), "cc")
            P.wait("gpsimd", "cc", 1)
            tap("state", state_sb[:, :], F32)
            tap("dlow", dlow_bf[:, :], BF16)
            tap("bc", bc_bf[:, :], BF16)

        P.barrier()
        if STOP == 2:
            return fin()

        with ExitStack() as ph:
            gath = ph.enter_context(nc.sbuf_tensor("gath", [128, 8 * 512], F32))
            gscr = ph.enter_context(nc.sbuf_tensor("gscr", [128, 1536], F32))
            lwq = dma("gpsimd", gath[:, :].rearrange("p (j c) -> p j c", j=8),
                      cc_out.ap().rearrange("(j p) c -> p j c", p=128), "lw")
            P.waitt("vector", lwq)
            scr = {"t1": gscr[:, 0:256], "p1": gscr[:, 256:512],
                   "t2": gscr[:, 512:768], "p2": gscr[:, 768:1024],
                   "t3": gscr[:, 1024:1280]}
            for name, ti, off in [("t1", 0, 256), ("p1", 0, 0), ("t2", 1, 256),
                                  ("p2", 1, 0), ("t3", 2, 256)]:
                s = scr[name]
                for j in range(8):
                    if j == 0:
                        P.do("vector", lambda e, s=s, ti=ti, off=off: e.tensor_scalar_mul(
                            s, gath[:, off:off + 256], msel_sb[:, ti * 8:ti * 8 + 1]), "v0")
                    else:
                        P.do("vector", lambda e, s=s, ti=ti, off=off, j=j:
                             e.scalar_tensor_tensor(
                            s, gath[:, j * 512 + off:j * 512 + off + 256],
                            msel_sb[:, ti * 8 + j:ti * 8 + j + 1], s,
                            op0=ALU.mult, op1=ALU.add), "v0")
            P.do("vector", lambda e: e.tensor_mul(
                gscr[:, 1280:1536], scr["p2"], scr["t3"]), "v0")
            P.do("vector", lambda e: e.tensor_add(
                gscr[:, 1280:1536], scr["t2"], gscr[:, 1280:1536]), "v0")
            P.do("vector", lambda e: e.tensor_mul(
                gscr[:, 1280:1536], scr["p1"], gscr[:, 1280:1536]), "v0")
            hin_done = P.do("vector", lambda e: e.tensor_add(
                hin_sb[:, :], scr["t1"], gscr[:, 1280:1536]), "v0")
            tap("gath", gath[:, :], F32)
            tap("hin", hin_sb[:, :], F32)
        P.barrier()
        if STOP == 3:
            return fin()

        mid.close()
        with ExitStack() as ph:
            def sbf(name, shape, dt=F32):
                return ph.enter_context(nc.sbuf_tensor(name, shape, dt))
            hbuf = sbf("hbuf", [128, HM * GT], BF16)
            xn_g = sbf("xn_g", [128, DC * GT], BF16)
            fcs = sbf("fcs", [128, 2 * 4096], BF16)
            prjs = sbf("prjs", [128, 3 * 2048], BF16)
            m_pat = sbf("m_pat", [128, GT], BF16)
            stage_e = sbf("stage_e", [128, 4 * 512])
            ccint = sbf("ccint", [128, 16 * 512], BF16)
            dcor = sbf("dcor", [128, 512])
            Dbuf = sbf("Dbuf", [128, 512])
            gbufs = sbf("gbufs", [128, 16 * 512], BF16)
            tv = sbf("tv", [128, 512])
            tg = sbf("tg", [128, 512])
            ycor_v = sbf("ycor_v", [128, 512])
            ycor_g = sbf("ycor_g", [128, 512])
            y2c = sbf("y2c", [128, 4 * 512], BF16)
            ssld = sbf("ssld", [128, 512], BF16)
            dqcarry = sbf("dqcarry", [128, ET])
            wTc = sbf("wTc", [128, 4 * 1024], BF16)
            silc = tv
            y2c_buf, tbst_buf = Buf(), Buf()
            ssld_buf = Buf()
            fcs_rel = {0: Buf(), 1: Buf()}
            prjs_rel = {0: Buf(), 1: Buf(), 2: Buf()}
            xng_buf, stgE_buf, hbuf_buf, mpat_buf = Buf(), Buf(), Buf(), Buf()

            def ffn_gen():
                for g2 in range(NG):
                    xng_buf.require("sync")
                    lq = dma("sync", xn_g[:, :].rearrange("p (c t) -> p c t", c=DC),
                             xn_hbm.ap()[:, :, g2 * GT:(g2 + 1) * GT].rearrange(
                                 "c p t -> p c t"), "la")
                    for ex in range(NEXP):
                        gq_m = {}
                        mpat_buf.require("vector")
                        for th in range(2):
                            bguard(4)
                            tq = P.do("tensor", lambda e, ex=ex, g2=g2, th=th: e.matmul(
                                bF(4), onem_sb[:, ex * 128:(ex + 1) * 128],
                                m_fm[:, g2 * GT + th * 512:g2 * GT + (th + 1) * 512],
                                start=True, stop=True), "t0")
                            P.wait("vector", "t0", tq)
                            P.do("vector", lambda e, th=th: e.tensor_copy(
                                m_pat[:, th * 512:(th + 1) * 512], bF(4)), "v0")
                            brelease("v0", 4)
                        yield
                        P.waitt("tensor", lq)
                        hbuf_buf.require("scalar")
                        for ms in range(8):
                            sh = ms % 2
                            fcs_rel[sh].require("sync")
                            lq2 = dma("sync",
                                      fcs[:, sh * 4096:(sh + 1) * 4096]
                                      .rearrange("p (kc mc q) -> p kc mc q", kc=8, mc=4),
                                      fc_bf.ap()[ex, :, ms * 512:(ms + 1) * 512]
                                      .rearrange("(kc p) (mc q) -> p kc mc q",
                                                 p=128, q=128), "la")
                            P.waitt("tensor", lq2)
                            for mc in range(4):
                                m = ms * 4 + mc
                                b0 = 4 + (m % 2) * 2
                                bguard(b0, b0 + 1)
                                for kc in range(8):
                                    for th in range(2):
                                        P.do("tensor", lambda e, kc=kc, mc=mc, th=th,
                                             b0=b0, sh=sh: e.matmul(
                                            bF(b0 + th),
                                            fcs[:, sh * 4096 + (kc * 4 + mc) * 128:
                                                sh * 4096 + (kc * 4 + mc) * 128 + 128],
                                            xn_g[:, kc * GT + th * 512:
                                                 kc * GT + (th + 1) * 512],
                                            start=(kc == 0), stop=(kc == 7)), "t0")
                                tq = P.cnt["t0"]
                                P.wait("scalar", "t0", tq)
                                for th in range(2):
                                    P.do("scalar", lambda e, m=m, th=th, b0=b0:
                                         e.activation(
                                        hbuf[:, m * GT + th * 512:m * GT + (th + 1) * 512],
                                        bF(b0 + th), AF.Gelu), "s0")
                                gq_m[m] = P.cnt["s0"]
                                brelease("s0", b0, b0 + 1)
                                yield
                            fcs_rel[sh].consumed("t0")
                        for th in range(2):
                            for dh in range(2):
                                bguard(4, 5, 6, 7)
                                pj_tok = {}

                                def pj_load(hs, dh=dh, ex=ex):
                                    sh = hs % 3
                                    prjs_rel[sh].require("sync")
                                    pj_tok[hs] = dma(
                                        "sync",
                                        prjs[:, sh * 2048:(sh + 1) * 2048]
                                        .rearrange("p (hc dc q) -> p hc dc q",
                                                   hc=4, dc=4),
                                        proj_bf.ap()[ex, hs * 512:(hs + 1) * 512,
                                                     dh * 512:(dh + 1) * 512]
                                        .rearrange("(hc p) (dc q) -> p hc dc q",
                                                   p=128, q=128), "la")

                                pj_load(0)
                                pj_load(1)
                                for hs in range(8):
                                    P.wait("tensor", "s0", gq_m[hs * 4 + 3])
                                    sh = hs % 3
                                    if hs + 2 < 8:
                                        pj_load(hs + 2)
                                    P.waitt("tensor", pj_tok[hs])
                                    for hc in range(4):
                                        m = hs * 4 + hc
                                        for dcg in range(4):
                                            P.do("tensor", lambda e, m=m, th=th, hc=hc,
                                                 dcg=dcg, sh=sh: e.matmul(
                                                bF(4 + dcg),
                                                prjs[:, sh * 2048 + (hc * 4 + dcg) * 128:
                                                     sh * 2048 + (hc * 4 + dcg) * 128 + 128],
                                                hbuf[:, m * GT + th * 512:
                                                     m * GT + (th + 1) * 512],
                                                start=(m == 0), stop=(m == 31)), "t0")
                                    prjs_rel[sh].consumed("t0")
                                    yield
                                tq = P.cnt["t0"]
                                P.wait("vector", "t0", tq)
                                stgE_buf.require("vector")
                                for dc4 in range(4):
                                    P.do("vector", lambda e, dc4=dc4, th=th: e.tensor_mul(
                                        stage_e[:, dc4 * 512:(dc4 + 1) * 512],
                                        bF(4 + dc4),
                                        m_pat[:, th * 512:(th + 1) * 512]), "v0")
                                vq = P.cnt["v0"]
                                brelease("v0", 4, 5, 6, 7)
                                P.wait("gpsimd", "v0", vq)
                                sdq = dma_add(("acc", g2, th, dh),
                                              acc_hbm.ap()[g2, dh * 4:(dh + 1) * 4, :,
                                                           th * 512:(th + 1) * 512]
                                              .rearrange("c p t -> p c t"),
                                              stage_e[:, :].rearrange(
                                                  "p (c t) -> p c t", c=4),
                                              first=(ex == 0))
                                stgE_buf.rel = sdq
                                yield
                        hbuf_buf.consumed("t0")
                        mpat_buf.consumed("v0")
                    xng_buf.consumed("t0")

            _gen = ffn_gen()
            _done = [False]

            def ffn_step(k=1):
                for _ in range(k):
                    if _done[0]:
                        return
                    if next(_gen, "END") == "END":
                        _done[0] = True
                        return

            wtc_free = [0]
            P.wait_fam("sync", "lw")
            P.wait("vector", "v0", hin_done)
            P.do("vector", lambda e: e.memset(dqcarry[:, :], 0.0), "v0")
            for q in range(NQ):
                qc = q * QT
                for n in range(NST):
                    row = 16 + n
                    bk = 2 + (n % 2)
                    bguard(bk)
                    tq = P.do("tensor", lambda e, row=row, qc=qc, bk=bk: e.matmul(
                        bF(bk), oneh_sb[:, row * 128:(row + 1) * 128],
                        bc_bf[:, qc:qc + 512], start=True, stop=True), "t0")
                    P.wait("vector", "t0", tq)
                    P.do("vector", lambda e, n=n, bk=bk: e.tensor_copy(
                        ccint[:, n:16 * 512:16], bF(bk)), "v0")
                    brelease("v0", bk)
                    ffn_step(2)
                for blk in range(4):
                    y2c_buf.require("vector")
                    if wtc_free[0]:
                        P.wait("sync", "t0", wtc_free[0])
                    lwc = dma("sync", wTc[:, :].rearrange("p (e k) -> p e k", e=4),
                              outT_hbm.ap()[blk * 4:(blk + 1) * 4].rearrange(
                                  "e p k -> p e k"), "la")
                    for ei in range(4):
                        et = blk * 4 + ei
                        bguard(0)
                        P.do("tensor", lambda e, et=et, qc=qc: e.matmul(
                            bF(0), wd2T[:, et * 128:(et + 1) * 128],
                            dlow_bf[:, qc:qc + 512], start=True, stop=True), "t0")
                        tq = P.cnt["t0"]
                        ffn_step(4)
                        P.wait("scalar", "t0", tq)
                        P.wait("scalar", "v0")
                        P.do("scalar", lambda e, et=et: e.activation(
                            silc[:, :], bF(0), AF.Abs, bias=wd2b_sb[:, et:et + 1]), "s0")
                        P.do("scalar", lambda e, et=et: e.activation(
                            dcor[:, :], bF(0), AF.Relu, bias=wd2b_sb[:, et:et + 1]), "s0")
                        brelease("s0", 0)
                        P.do("scalar", lambda e: e.activation(
                            silc[:, :], silc[:, :], AF.Exp, scale=-1.0), "s0")
                        P.do("scalar", lambda e: e.activation(
                            silc[:, :], silc[:, :], AF.Ln, bias=1.0), "s0")
                        P.wait("vector", "s0")
                        P.do("vector", lambda e: e.tensor_add(
                            dcor[:, :], silc[:, :], dcor[:, :]), "v0")
                        init = 0.0 if q == 0 else dqcarry[:, et:et + 1]
                        P.do("vector", lambda e, init=init: e.tensor_tensor_scan(
                            Dbuf[:, :], onesf[:, :], dcor[:, :], init,
                            op0=ALU.mult, op1=ALU.add), "v0")
                        vqD = P.do("vector", lambda e, et=et: e.tensor_copy(
                            dqcarry[:, et:et + 1], Dbuf[:, 511:512]), "v0")
                        ssld_buf.require("sync")
                        sld = dma("sync", ssld[:, :], ss_hbm.ap()[et, :, qc:qc + 512], "la")
                        ffn_step(4)
                        P.wait("scalar", "v0", vqD)
                        P.wait("scalar", "g0")
                        for n in range(NST):
                            P.do("scalar", lambda e, et=et, n=n: e.activation(
                                gbufs[:, n * 512:(n + 1) * 512], Dbuf[:, :], AF.Exp,
                                scale=a_sb[:, et * NST + n:et * NST + n + 1]), "s0")
                        sq_all = P.cnt["s0"]
                        ffn_step(4)
                        P.wait("vector", "s0", sq_all)
                        P.wait("vector", "g0")
                        gtok = [0]
                        for n in range(NST):
                            dst = (ycor_v if n == 0 else
                                   ycor_g if n == 1 else
                                   tv if n % 2 == 0 else tg)
                            if n >= 3 and n % 2 == 1 and gtok[0]:
                                P.wait("vector", "g0", gtok[0])
                            vq = P.do("vector", lambda e, n=n, et=et, dst=dst:
                                 e.scalar_tensor_tensor(
                                dst[:, :],
                                gbufs[:, n * 512:(n + 1) * 512],
                                hin_sb[:, et * NST + n:et * NST + n + 1],
                                ccint[:, n * 512:(n + 1) * 512],
                                op0=ALU.mult, op1=ALU.mult), "v0")
                            if n >= 2 and n % 2 == 0:
                                P.do("vector", lambda e: e.tensor_add(
                                    ycor_v[:, :], ycor_v[:, :], tv[:, :]), "v0")
                            elif n >= 3:
                                P.wait("gpsimd", "v0", vq)
                                gtok[0] = P.do(
                                    "gpsimd", lambda e: e.tensor_tensor(
                                        ycor_g[:, :], ycor_g[:, :], tg[:, :],
                                        op=ALU.add), "g0")
                        P.wait("vector", "g0")
                        P.do("vector", lambda e: e.tensor_add(
                            ycor_v[:, :], ycor_v[:, :], ycor_g[:, :]), "v0")
                        P.waitt("vector", sld)
                        P.do("vector", lambda e, ei=ei: e.tensor_mul(
                            y2c[:, ei * 512:(ei + 1) * 512], ycor_v[:, :], ssld[:, :]), "v0")
                        ssld_buf.consumed("v0")
                        ffn_step(4)
                    P.wait("tensor", "v0")
                    P.waitt("tensor", lwc)
                    for g2c in range(2):
                        bguard(0, 1, 2, 3)
                        for ei in range(4):
                            for dc in range(DC):
                                P.do("tensor", lambda e, ei=ei, dc=dc, g2c=g2c: e.matmul(
                                    PS1[:, dc * 256:(dc + 1) * 256],
                                    wTc[:, ei * 1024 + dc * 128:ei * 1024 + (dc + 1) * 128],
                                    y2c[:, ei * 512 + g2c * 256:ei * 512 + (g2c + 1) * 256],
                                    start=(ei == 0 and dc % 2 == 0),
                                    stop=(ei == 3 and dc % 2 == 1)), "t0")
                        tq = P.cnt["t0"]
                        wtc_free[0] = tq
                        y2c_buf.consumed("t0")
                        P.wait("vector", "t0", tq)
                        stgE_buf.require("vector")
                        vq = P.do("vector", lambda e: e.tensor_copy(
                            stage_e[:, 0:2048], PS1[:, :]), "v0")
                        brelease("v0", 0, 1, 2, 3)
                        P.wait("gpsimd", "v0", vq)
                        sdq = dma_add(("s", q, g2c),
                                      s_hbm.ap()[:, :, qc + g2c * 256:qc + (g2c + 1) * 256]
                                      .rearrange("c p t -> p c t"),
                                      stage_e[:, 0:2048].rearrange(
                                          "p (c t) -> p c t", c=DC),
                                      first=False)
                        stgE_buf.rel = sdq
                        ffn_step(2)
            while not _done[0]:
                ffn_step(8)

        P.barrier()
        if STOP == 5:
            return fin()

        with ExitStack() as ph:
            def sbf(name, shape, dt=F32):
                return ph.enter_context(nc.sbuf_tensor(name, shape, dt))
            accb = sbf("accb", [128, DC * 512])
            tb = sbf("tb", [128, DC * 512])
            sbs = sbf("sbs", [128, DC * 512])
            xnc = sbf("xnc", [128, DC * 512], BF16)
            rtl = sbf("rtl", [128, D])
            outsb = sbf("outsb", [128, D])
            out_buf, rtl_buf = Buf(), Buf()
            tbst_buf = Buf()
            accb_buf, sbs_buf, xnc_buf, tb_buf = Buf(), Buf(), Buf(), Buf()

            P.wait_fam("sync", "sd")
            for g2 in range(NG):
                for th in range(2):
                    t0c = g2 * GT + th * 512
                    accb_buf.require("sync")
                    sbs_buf.require("sync")
                    xnc_buf.require("sync")
                    tb_buf.require("scalar")
                    l1 = dma("sync", accb[:, :].rearrange("p (c t) -> p c t", c=DC),
                             acc_hbm.ap()[g2, :, :, th * 512:(th + 1) * 512].rearrange(
                                 "c p t -> p c t"), "la")
                    l2 = dma("sync", sbs[:, :].rearrange("p (c t) -> p c t", c=DC),
                             s_hbm.ap()[:, :, t0c:t0c + 512].rearrange(
                                 "c p t -> p c t"), "la")
                    l3 = dma("sync", xnc[:, :].rearrange("p (c t) -> p c t", c=DC),
                             xn_hbm.ap()[:, :, t0c:t0c + 512].rearrange(
                                 "c p t -> p c t"), "la")
                    P.waitt("scalar", l1)
                    P.do("scalar", lambda e: e.activation(
                        tb[:, :], accb[:, :], AF.Silu), "s0")
                    P.wait("vector", "s0")
                    P.waitt("vector", l2)
                    P.waitt("vector", l3)
                    P.do("vector", lambda e: e.tensor_add(
                        sbs[:, :], sbs[:, :], xnc[:, :]), "v0")
                    P.do("vector", lambda e: e.tensor_mul(
                        sbs[:, :], sbs[:, :], tb[:, :]), "v0")
                    vq_sg = P.cnt["v0"]
                    accb_buf.consumed("s0")
                    xnc_buf.consumed("v0")
                    tb_buf.consumed("v0")
                    P.wait("tensor", "v0", vq_sg)
                    for tt in range(4):
                        rrow = t0c + tt * 128
                        rtl_buf.require("sync")
                        lr = dma("sync", rtl[:, :],
                                 resid_m.ap()[rrow:rrow + 128, :], "la")
                        for dh in range(2):
                            bguard(2)
                            for dc4 in range(4):
                                dcg = dh * 4 + dc4
                                P.do("tensor", lambda e, dcg=dcg, dc4=dc4, tt=tt: e.matmul(
                                    bF(2)[:, dc4 * 128:(dc4 + 1) * 128],
                                    sbs[:, dcg * 512 + tt * 128:dcg * 512 + (tt + 1) * 128],
                                    idf_sb[:, :], start=(dc4 == 0), stop=(dc4 == 3)), "t0")
                            tq = P.cnt["t0"]
                            P.wait("vector", "t0", tq)
                            P.waitt("vector", lr)
                            out_buf.require("vector")
                            P.do("vector", lambda e, dh=dh: e.tensor_add(
                                outsb[:, dh * 512:(dh + 1) * 512], bF(2),
                                rtl[:, dh * 512:(dh + 1) * 512]), "v0")
                            brelease("v0", 2)
                        rtl_buf.consumed("v0")
                        sbs_buf.consumed("t0")
                        vq = P.cnt["v0"]
                        P.wait("sync", "v0", vq)
                        oq = dma("sync", out_t.ap()[rrow:rrow + 128, :], outsb[:, :], "ob")
                        out_buf.rel = oq

            P.wait_fam("sync", "ob")
            P.wait_fam("gpsimd", "sd")

        block = top.enter_context(nc.Block())
        P.replay(block)

    return nc


_IDENT_F = np.eye(128, dtype=np.float32)
_IDENT_B = np.eye(128, dtype=bfdt)


def _onehot_blocks(nblk, row_of):
    o = np.zeros((128, nblk * 128), dtype=bfdt)
    for j in range(nblk):
        o[row_of(j), j * 128:(j + 1) * 128] = 1.0
    return o


def kernel(L0_attn_resid, L0_attn_norm_w, L0_attn_skip_w, L0_attn_in_w,
           L0_attn_conv_w, L0_attn_conv_b, L0_attn_Wd1, L0_attn_Wd2_w,
           L0_attn_Wd2_b, L0_attn_WB, L0_attn_WC, L0_attn_A_log, L0_attn_W_D,
           L0_attn_out_w, L0_ffn_gate_w, L0_ffn_fc_w, L0_ffn_proj_w):
    resid = np.ascontiguousarray(np.asarray(L0_attn_resid, dtype=np.float32))
    assert resid.shape == (2, 8192, 1024)

    stack_w = np.ascontiguousarray(np.concatenate(
        [np.asarray(L0_attn_Wd1, np.float32),
         np.asarray(L0_attn_WB, np.float32),
         np.asarray(L0_attn_WC, np.float32)], axis=0))
    conv_w2 = np.ascontiguousarray(
        np.asarray(L0_attn_conv_w, np.float32).reshape(E, DCONV))

    common = dict(
        norm_w=np.ascontiguousarray(L0_attn_norm_w, dtype=np.float32),
        in_w=np.ascontiguousarray(L0_attn_in_w, dtype=np.float32),
        skip_w=np.ascontiguousarray(L0_attn_skip_w, dtype=np.float32),
        out_w=np.ascontiguousarray(L0_attn_out_w, dtype=np.float32),
        conv_w=conv_w2,
        conv_b=np.ascontiguousarray(L0_attn_conv_b, dtype=np.float32),
        stack_w=stack_w,
        wd2_w=np.ascontiguousarray(L0_attn_Wd2_w, dtype=np.float32),
        wd2_b=np.ascontiguousarray(L0_attn_Wd2_b, dtype=np.float32),
        a_log=np.ascontiguousarray(L0_attn_A_log, dtype=np.float32),
        w_d=np.ascontiguousarray(L0_attn_W_D, dtype=np.float32),
        gate_w=np.ascontiguousarray(L0_ffn_gate_w, dtype=np.float32),
        fc_w=np.ascontiguousarray(L0_ffn_fc_w, dtype=np.float32),
        proj_w=np.ascontiguousarray(L0_ffn_proj_w, dtype=np.float32),
        ident_f=_IDENT_F, ident_b=_IDENT_B,
        oneh_bf=_onehot_blocks(32, lambda j: 64 + j),
        onem_bf=_onehot_blocks(8, lambda j: j),
    )

    in_maps = []
    for c in range(NCORE):
        sq, ck = divmod(c, 4)
        r0 = ck * T
        rm = np.ascontiguousarray(resid[sq, r0:r0 + T, :])
        rh = np.zeros((128, D), dtype=np.float32)
        if ck > 0:
            rh[125:128, :] = resid[sq, r0 - 3:r0, :]
        ms = np.zeros((128, 24), dtype=np.float32)
        for ti in range(3):
            j = c - 1 - ti
            if ck - 1 - ti >= 0:
                ms[:, ti * 8 + j] = 1.0
        im = dict(common)
        im["resid_m"] = rm
        im["resid_h"] = rh
        im["msel"] = ms
        in_maps.append(im)

    nc = build_program()
    res = run_bass_kernel_spmd(nc, in_maps, core_ids=list(range(NCORE)))

    out = np.empty((2, 8192, 1024), dtype=np.float32)
    for c in range(NCORE):
        sq, ck = divmod(c, 4)
        out[sq, ck * T:(ck + 1) * T, :] = res.results[c]["out"]
    return out
